# version 09
# speedup vs baseline: 1.0399x; 1.0188x over previous
import contextlib
import math

import numpy as np
import ml_dtypes

import concourse.bass as bass
import concourse.mybir as mybir
from concourse.bass_utils import run_bass_kernel_spmd

F32 = mybir.dt.float32
BF16 = mybir.dt.bfloat16
AF = mybir.ActivationFunctionType
ALU = mybir.AluOpType
AX = mybir.AxisListType
bf16 = ml_dtypes.bfloat16

B, S, D = 4, 4096, 2048
NCORES = 8
H0, DH0 = 16, 128
WINDOW = 1024
SOFTCAP = 30.0
SCALE0 = 1.0 / math.sqrt(DH0)
DFF = 8192
QLR, KVLR = 1536, 512
H1, DN, DR, DV = 16, 128, 64, 128
SCALE1 = 1.0 / math.sqrt(DN + DR)
EPS_LN = 1e-5
EPS_RMS = 1e-6
E_MOE, F_MOE = 8, 768
NEG = -1e30

ENGINES = ("sync", "vector", "scalar", "tensor", "gpsimd")


class Prog:

    def __init__(self, nc, es):
        self.nc = nc
        self.es = es
        self.recs = {e: [] for e in ENGINES}
        self.semh = {}
        self.cnt = {}
        self.hw = {}
        self.last_dma = {}

    def sem(self, name):
        if name not in self.semh:
            self.semh[name] = self.es.enter_context(self.nc.semaphore(name))
            self.cnt[name] = 0
        return self.semh[name]

    def op(self, eng, build, after=(), inc=None, inck=1):
        tok = None
        if inc is not None:
            self.sem(inc)
            self.cnt[inc] += inck
            tok = (inc, self.cnt[inc], eng)
        after = tuple(t for t in after if t is not None)
        self.recs[eng].append((build, after, (inc, inck) if inc else None))
        return tok

    def dma(self, eng, out, in_, after=(), inc=None, chain=True, **kw):
        assert inc is not None, "dma needs an explicit semaphore"
        def build(e):
            return e.dma_start(out=out, in_=in_, **kw)
        self.sem(inc)
        after = list(after)
        if chain:
            after.append(self.last_dma.get(inc))
        self.cnt[inc] += 16
        tok = (inc, self.cnt[inc], None)
        self.last_dma[inc] = tok
        after = tuple(t for t in after if t is not None)
        self.recs[eng].append((build, after, (inc, 16)))
        return tok

    def barrier(self, eng, toks):
        self.recs[eng].append((None, tuple(t for t in toks if t is not None), None))

    def stage_barrier(self):
        toks = tuple((s, v, None) for s, v in self.cnt.items() if v > 0)
        for eng in ENGINES:
            self.recs[eng].append((None, toks, None))

    def emit(self, block):
        hw = self.hw
        for eng in ENGINES:
            if not self.recs[eng]:
                continue
            recs = self.recs[eng]
            semh = self.semh

            def section(e, recs=recs, eng=eng):
                for build, after, inc in recs:
                    for s, v, src in after:
                        if src == "tensor" and eng == "tensor":
                            continue
                        if hw.get((eng, s), 0) >= v:
                            continue
                        e.wait_ge(semh[s], v)
                        hw[(eng, s)] = v
                    if build is None:
                        continue
                    instr = build(e)
                    if inc is not None:
                        instr.then_inc(semh[inc[0]], inc[1])

            getattr(block, eng)(section)
        self.recs = {e: [] for e in ENGINES}


class SB:
    pass


class OneSlot:

    def __init__(self):
        self.tok = None

    def __getitem__(self, i):
        return self.tok

    def __setitem__(self, i, v):
        self.tok = v


def tsl(i, n=128):
    return slice(n * i, n * (i + 1))


def stage_shared(inp):
    d = {}
    d["ident"] = np.eye(128, dtype=bf16)
    for nm, key in (("wqb", "L0_attn_w_q"), ("wkb", "L0_attn_w_k"),
                    ("wvb", "L0_attn_w_v")):
        w = np.asarray(inp[key], np.float32)
        d[nm] = np.ascontiguousarray(
            w.T.reshape(D, D // 128, 128).transpose(1, 0, 2))
    d["woT"] = np.ascontiguousarray(np.asarray(inp["L0_attn_w_o"], np.float32).T)
    upT = np.asarray(inp["L0_ffn_up_weight"], np.float32).T
    d["up_wb"] = np.ascontiguousarray(
        upT.reshape(D, DFF // 128, 128).transpose(1, 0, 2))
    d["down_wT"] = np.ascontiguousarray(
        np.asarray(inp["L0_ffn_down_weight"], np.float32).T)
    us = np.asarray(inp["L0_ffn_up_scale"], np.float32)
    d["us2d"] = np.ascontiguousarray(us.reshape(DFF // 128, 128).T)
    d["ds_t"] = np.ascontiguousarray(
        np.broadcast_to(np.asarray(inp["L0_ffn_down_scale"], np.float32), (128, D)))
    d["os_col"] = np.broadcast_to(
        np.asarray(inp["L0_ffn_output_scale"], np.float32).reshape(1, 1),
        (128, 1)).copy()
    ii = np.arange(128)[:, None]
    jj = np.arange(128)[None, :]
    d["tri_first_add"] = np.where(jj > ii, 0.0, NEG).astype(bf16)
    d["tri_last_add"] = np.where(jj <= ii, 0.0, NEG).astype(bf16)
    npadc = np.zeros((S // 128, 128), np.float32)
    for qt in range(S // 128):
        for r in range(128):
            i = 128 * qt + r
            npadc[qt, r] = -max(0, (WINDOW - 1) - i) * math.exp(-SOFTCAP)
    d["npadcT"] = np.ascontiguousarray(npadc.T)
    d["wq_a"] = np.ascontiguousarray(np.asarray(inp["L1_attn_wq_a"], np.float32))
    d["gq_2d"] = np.ascontiguousarray(
        np.asarray(inp["L1_attn_gq_a"], np.float32).reshape(QLR // 128, 128).T)
    d["wq_b"] = np.ascontiguousarray(np.asarray(inp["L1_attn_wq_b"], np.float32))
    d["wkv_a"] = np.ascontiguousarray(np.asarray(inp["L1_attn_wkv_a"], np.float32))
    d["gkv_2d"] = np.ascontiguousarray(
        np.asarray(inp["L1_attn_gkv_a"], np.float32).reshape(KVLR // 128, 128).T)
    wkv_b = np.asarray(inp["L1_attn_wkv_b"], np.float32).reshape(KVLR, H1, DN + DV)
    d["wkv_b_k"] = np.ascontiguousarray(wkv_b[:, :, :DN].reshape(KVLR, H1 * DN))
    d["wkv_b_v"] = np.ascontiguousarray(wkv_b[:, :, DN:].reshape(KVLR, H1 * DV))
    d["wo1"] = np.ascontiguousarray(np.asarray(inp["L1_attn_wo"], np.float32))
    d["router_T"] = np.ascontiguousarray(
        np.asarray(inp["L1_ffn_w_gate_router"], np.float32).T)
    d["wg_T"] = np.ascontiguousarray(
        np.transpose(np.asarray(inp["L1_ffn_w_gate_proj"], np.float32), (0, 2, 1)))
    d["wu_T"] = np.ascontiguousarray(
        np.transpose(np.asarray(inp["L1_ffn_w_up_proj"], np.float32), (0, 2, 1)))
    d["wd_T"] = np.ascontiguousarray(
        np.transpose(np.asarray(inp["L1_ffn_w_down_proj"], np.float32), (0, 2, 1)))
    return d


def own_blocks(parity):
    return [2 * j + parity for j in range(S // 256)]


def stage_x0(inp, core):
    b = core // 2
    x = np.asarray(inp["L0_attn_x"], np.float32)[b]
    xpad = np.zeros((WINDOW + S, D), np.float32)
    xpad[WINDOW:] = x
    return xpad


def emit_ln_tile(p, sb, x_ap, x_tok, prev, width, eps=EPS_LN):
    ngr = width // 512
    t_bn = None
    for r in range(ngr):
        t_bn = p.op("vector", lambda v, r=r, x_ap=x_ap: v.bn_stats(
            out=sb.stats6[:, 6 * r:6 * (r + 1)], in_=x_ap[:, 512 * r:512 * (r + 1)]),
            after=[x_tok, prev.get("ag"), prev.get("eps")] if r == 0 else [],
            inc="VE" if r == ngr - 1 else None)
    t_ag = p.op("vector", lambda v: v.bn_aggr(
        out=sb.mv[:, :], in_=sb.stats6[:, :6 * ngr]), after=[t_bn, prev.get("ln"), prev.get("xn")], inc="VE")
    prev["ag"] = t_ag
    t_eps = p.op("vector", lambda v: v.tensor_scalar_add(
        out=sb.veps[:, :], in0=sb.mv[:, 1:2], scalar1=float(eps)),
        after=[t_ag, prev.get("ln")], inc="VE")
    prev["eps"] = t_eps
    t_ln = p.op("scalar", lambda s: s.activation(
        out=sb.rstd[:, :], in_=sb.veps[:, :], func=AF.Ln),
        after=[t_eps, prev.get("xn")], inc="SC")
    prev["ln"] = t_ln
    t_ex = p.op("scalar", lambda s: s.activation(
        out=sb.rstd[:, :], in_=sb.rstd[:, :], func=AF.Exp, scale=-0.5),
        after=[t_ln], inc="SC")
    t_xn = p.op("vector", lambda v, x_ap=x_ap: v.tensor_scalar(
        out=sb.xn[:, :width], in0=x_ap, scalar1=sb.mv[:, 0:1],
        scalar2=sb.rstd[:, 0:1], op0=ALU.subtract, op1=ALU.mult),
        after=[t_ex, prev.get("trlast")], inc="VE")
    prev["xn"] = t_xn
    return t_xn


def emit_transpose_tile(p, sb, t_xn, width, dsts, slot_busy, ident_tok=None,
                        extra0=()):
    nch = width // 128
    t_ev = None
    t_tr = None
    for c in range(nch):
        par = c % 2
        pt = sb.psT[par]
        t_tr = p.op("tensor", lambda t, c=c, pt=pt: t.transpose(
            out=pt[:, :], in_=sb.xn[:, tsl(c)], identity=sb.ident[:, :]),
            after=[t_xn, slot_busy[par]] +
                  (([ident_tok] + list(extra0)) if c == 0 else []),
            inc="TE")
        dlist = dsts(c)
        for k, dst in enumerate(dlist):
            t_ev = p.op("vector", lambda v, pt=pt, dst=dst: v.tensor_copy(
                out=dst, in_=pt[:, :]), after=[t_tr], inc="VE")
        slot_busy[par] = t_ev
    return t_ev, t_tr


def emit_s2(p, es, ap, seq=S, dff=DFF, first=()):
    nc = p.nc
    TG = 512
    NG = seq // TG
    NT = TG // 128
    NHC = 16
    NSUP = dff // (128 * NHC)
    NCH = D // 128
    NDG = D // 512

    sb = SB()
    sb.ident = es.enter_context(nc.sbuf_tensor("s2_ident", [128, 128], BF16))
    sb.x = es.enter_context(nc.sbuf_tensor("s2_x", [128, 2, D], F32))
    sb.stats6 = es.enter_context(nc.sbuf_tensor("s2_st6", [128, 24], F32))
    sb.mv = es.enter_context(nc.sbuf_tensor("s2_mv", [128, 2], F32))
    sb.veps = es.enter_context(nc.sbuf_tensor("s2_veps", [128, 1], F32))
    sb.rstd = es.enter_context(nc.sbuf_tensor("s2_rstd", [128, 1], F32))
    sb.xn = es.enter_context(nc.sbuf_tensor("s2_xn", [128, D], BF16))
    sb.xnT = [es.enter_context(nc.sbuf_tensor(f"s2_xnT{i}", [128, NCH, TG], BF16))
              for i in range(2)]
    sb.us = es.enter_context(nc.sbuf_tensor("s2_us", [128, dff // 128], F32))
    sb.ds = es.enter_context(nc.sbuf_tensor("s2_ds", [128, D], F32))
    sb.os = es.enter_context(nc.sbuf_tensor("s2_os", [128, 1], F32))
    NUB = 4
    sb.ub = [es.enter_context(nc.sbuf_tensor(f"s2_ub{i}", [128, NCH, 128], BF16))
             for i in range(NUB)]
    sb.db = [es.enter_context(nc.sbuf_tensor("s2_db0", [128, NHC, D], BF16))] * 2
    sb.hid = es.enter_context(nc.sbuf_tensor("s2_hid", [128, NHC, TG], BF16))
    sb.acc = es.enter_context(nc.sbuf_tensor("s2_acc", [128, NT, D], F32))
    sb.tmp = es.enter_context(nc.sbuf_tensor("s2_tmp", [128, D], F32))
    sb.psT = [es.enter_context(nc.psum_tensor(f"s2_psT{i}", [128, 128], BF16))
              for i in range(2)]
    sb.psU = [es.enter_context(nc.psum_tensor(f"s2_psU{i}", [128, TG], F32))
              for i in range(2)]
    sb.psD = [es.enter_context(nc.psum_tensor(f"s2_psD{i}", [128, 512], F32))
              for i in range(4)]

    p.dma("sync", sb.ident[:, :], ap["ident"][:, :], after=first, inc="CONST")
    p.dma("sync", sb.us[:, :], ap["us2d"][:, :], after=first, inc="CONST")
    p.dma("sync", sb.ds[:, :], ap["ds_t"][:, :], after=first, inc="CONST")
    t_os = p.dma("sync", sb.os[:, :], ap["os_col"][:, :], after=first, inc="CONST")
    t_id = t_us = t_ds = t_os
    consts = [t_os]

    NH = dff // 128
    def up_src_g0(h):
        return ap["up_wb"][h].rearrange("(k p) m -> p k m", p=128)
    def up_cache(h):
        return ap["up_sgn_c"][h].rearrange("p (k m) -> p k m", k=NCH)

    ub_busy = [None] * 4
    ub_store = {}
    ub_store_last = [None] * 4
    db_store_all = None
    psT_busy = [None, None]
    psU_busy = [None, None]
    psD_busy = [None] * 4
    acc_last = [[None] * 4 for _ in range(NT)]
    acc_busy = [None] * NT
    x_busy = [None] * 2
    xnT_busy = [None, None]
    hid_db_busy = None
    dn_busy = [None, None]
    tmp_busy = None
    prev = {}
    last_st = None

    def emit_ln_group(g):
        gp = g % 2
        tr_done = []
        for i in range(NT):
            tl = p.dma("sync", sb.x[:, i % 2, :], ap["x1"][tsl(NT * g + i), :],
                       after=[x_busy[i % 2]] + (list(first) if g == 0 else []),
                       inc=f"LDX{i % 2}")
            t_xn = emit_ln_tile(p, sb, sb.x[:, i % 2, :], tl, prev, D)
            x_busy[i % 2] = t_xn
            t_ev, t_tr = emit_transpose_tile(
                p, sb, t_xn, D,
                lambda c, i=i, gp=gp: [sb.xnT[gp][:, c, tsl(i)]],
                psT_busy, ident_tok=t_id,
                extra0=[xnT_busy[gp]] if i == 0 else ())
            prev["trlast"] = t_tr
            tr_done.append(t_ev)
        return tr_done

    tr_done_g = {0: emit_ln_group(0)}
    for g in range(NG):
        gp = g % 2
        tr_done = tr_done_g.pop(g)

        ub_ready = {}

        def issue_ub_load(h):
            par = h % NUB
            ub = sb.ub[par]
            if g == 0:
                tl = p.dma("gpsimd", ub[:, :, :], up_src_g0(h),
                           after=[ub_busy[par], ub_store.get(h - NUB)],
                           inc=f"LWU{par}")
                tsg = p.op("scalar", lambda sc, ub=ub: sc.activation(
                    out=ub[:, :, :], in_=ub[:, :, :], func=AF.Sign),
                    after=[tl], inc="SC")
                tst = p.dma("sync", up_cache(h), ub[:, :, :],
                            after=[tsg], inc=f"STU{par}")
                ub_store[h] = tst
                ub_store_last[par] = tst
                ub_ready[h] = tsg
            else:
                war_st = [ub_store_last[par]] if (g == 1 and h < NUB) else []
                tl = p.dma("gpsimd", ub[:, :, :], up_cache(h),
                           after=[ub_busy[par]] + war_st, inc=f"LWU{par}")
                ub_ready[h] = tl

        for h0 in range(NUB):
            issue_ub_load(h0)
        for s_i in range(NSUP):
            hid_toks = []
            db_toks = []
            db_loads = []
            for hh in range(NHC):
                h = NHC * s_i + hh
                par = h % NUB
                ub = sb.ub[par]
                w_ready = ub_ready[h]
                pup = hh % 2
                pu = sb.psU[pup]
                t_mm = None
                for c in range(NCH):
                    t_mm = p.op("tensor", lambda t, c=c, pu=pu, ub=ub, gp=gp: t.matmul(
                        out=pu[:, :], lhsT=ub[:, c, :], rhs=sb.xnT[gp][:, c, :],
                        start=(c == 0), stop=(c == NCH - 1)),
                        after=([w_ready, psU_busy[pup]] + tr_done) if c == 0 else [],
                        inc="TE" if c == NCH - 1 else None)
                ub_busy[par] = t_mm
                xnT_busy[gp] = t_mm
                if h + NUB < NH:
                    issue_ub_load(h + NUB)
                t_hid = p.op("scalar", lambda sc, pu=pu, hh=hh, h=h: sc.activation(
                    out=sb.hid[:, hh, :], in_=pu[:, :], func=AF.Relu,
                    scale=sb.us[:, h:h + 1]),
                    after=[t_mm, hid_db_busy, t_us], inc="SC")
                psU_busy[pup] = t_hid
                hid_toks.append(t_hid)
                sp = s_i % 2
                if g == 0:
                    tdl = p.dma("gpsimd", sb.db[0][:, hh, :],
                                ap["down_wT"][tsl(h), :],
                                after=[hid_db_busy], inc="LWD", chain=False)
                    db_loads.append(tdl)
                else:
                    war_st = [db_store_all] if (g == 1 and s_i == 0 and hh == 0) else []
                    tdl = p.dma("gpsimd", sb.db[0][:, hh, :],
                                ap["down_sgn_c"][tsl(h), :],
                                after=[hid_db_busy] + war_st, inc="LWD", chain=False)
                    db_toks.append(tdl)
            if g == 0:
                allld = db_loads[-1]
                for hh in range(NHC):
                    h = NHC * s_i + hh
                    tdsg = p.op("scalar", lambda sc, hh=hh, sp=sp: sc.activation(
                        out=sb.db[sp][:, hh, :], in_=sb.db[sp][:, hh, :],
                        func=AF.Sign), after=[allld], inc="SC")
                    tdst = p.dma("sync", ap["down_sgn_c"][tsl(h), :],
                                 sb.db[sp][:, hh, :], after=[tdsg], inc="STD")
                    db_store_all = tdst
                    db_toks.append(tdsg)
            t_dn_last = None
            db_gate = db_toks if g == 0 else db_toks[-1:]
            for i in range(NT):
                for dg in range(NDG):
                    for hh in range(NHC):
                        t_dn_last = p.op("tensor", lambda t, i=i, hh=hh, dg=dg, sp=sp: t.matmul(
                            out=sb.psD[dg][:, :], lhsT=sb.hid[:, hh, tsl(i)],
                            rhs=sb.db[sp][:, hh, 512 * dg:512 * (dg + 1)],
                            start=(hh == 0), stop=(hh == NHC - 1)),
                            after=(hid_toks + db_gate + [psD_busy[dg]])
                            if hh == 0 else [],
                            inc="TE" if hh == NHC - 1 else None)
                    if s_i == 0:
                        t_ac = p.op("vector", lambda v, i=i, dg=dg: v.tensor_copy(
                            out=sb.acc[:, i, 512 * dg:512 * (dg + 1)],
                            in_=sb.psD[dg][:, :]),
                            after=[t_dn_last, acc_busy[i]], inc="VE")
                    else:
                        t_ac = p.op("vector", lambda v, i=i, dg=dg: v.tensor_tensor(
                            out=sb.acc[:, i, 512 * dg:512 * (dg + 1)],
                            in0=sb.acc[:, i, 512 * dg:512 * (dg + 1)],
                            in1=sb.psD[dg][:, :], op=ALU.add),
                            after=[t_dn_last, acc_last[i][dg]], inc="VE")
                    psD_busy[dg] = t_ac
                    acc_last[i][dg] = t_ac
            hid_db_busy = t_dn_last
            dn_busy[sp] = t_dn_last
            if s_i == 0 and g + 1 < NG:
                tr_done_g[g + 1] = emit_ln_group(g + 1)
        for i in range(NT):
            trl = p.dma("sync", sb.x[:, i % 2, :], ap["x1"][tsl(NT * g + i), :],
                        after=[x_busy[i % 2]], inc=f"LDX{i % 2}")
            accdone = [acc_last[i][dg] for dg in range(NDG)]
            t1 = p.op("vector", lambda v, i=i: v.tensor_tensor(
                out=sb.tmp[:, :], in0=sb.acc[:, i, :], in1=sb.ds[:, :],
                op=ALU.mult), after=accdone + [t_ds, tmp_busy], inc="VE")
            acc_busy[i] = t1
            t2 = p.op("vector", lambda v, i=i: v.scalar_tensor_tensor(
                out=sb.tmp[:, :], in0=sb.tmp[:, :], scalar=sb.os[:, 0:1],
                in1=sb.x[:, i % 2, :], op0=ALU.mult, op1=ALU.add),
                after=[t1, t_os, trl], inc="VE")
            x_busy[i % 2] = t2
            tst = p.dma("sync", ap["x2"][tsl(NT * g + i), :], sb.tmp[:, :],
                        after=[t2], inc="STT")
            tmp_busy = tst
            last_st = tst
    return [last_st]



def emit_s1(p, block, ap, seq=S, d=D, h0=H0, window=WINDOW):
    nc = p.nc
    SP = window + seq
    NTP = SP // 128
    NTQ = seq // 128
    NCH = d // 128
    WT = window // 128 + 1
    NKG = (WT + 3) // 4
    scw = WT * 128

    es = contextlib.ExitStack()
    with es:
        sb = SB()
        sb.ident = es.enter_context(nc.sbuf_tensor("s1_ident", [128, 128], BF16))
        sb.x = es.enter_context(nc.sbuf_tensor("s1_x", [128, 2, d], F32))
        sb.stats6 = es.enter_context(nc.sbuf_tensor("s1_st6", [128, 24], F32))
        sb.mv = es.enter_context(nc.sbuf_tensor("s1_mv", [128, 2], F32))
        sb.veps = es.enter_context(nc.sbuf_tensor("s1_veps", [128, 1], F32))
        sb.rstd = es.enter_context(nc.sbuf_tensor("s1_rstd", [128, 1], F32))
        sb.xn = es.enter_context(nc.sbuf_tensor("s1_xn", [128, d], BF16))
        sb.xnT = es.enter_context(nc.sbuf_tensor("s1_xnT", [128, NCH, SP], BF16))
        sb.wb = [es.enter_context(nc.sbuf_tensor(f"s1_wb{i}", [128, NCH, 128], BF16))
                 for i in range(3)]
        sb.st = [es.enter_context(nc.sbuf_tensor(f"s1_stg{i}", [128, 512], BF16))
                 for i in range(2)]
        sb.psT = [es.enter_context(nc.psum_tensor(f"s1_psT{i}", [128, 128], BF16))
                  for i in range(2)]
        sb.psP = [es.enter_context(nc.psum_tensor(f"s1_psP{i}", [128, 512], F32))
                  for i in range(2)]

        t_id = p.dma("sync", sb.ident[:, :], ap["ident"][:, :], inc="CONST")
        psT_busy = [None, None]
        x_busy = [None] * 2
        prev = {}
        tr_last_all = None
        for i in range(NTP):
            sl = i % 2
            tl = p.dma("sync", sb.x[:, sl, :], ap["x0"][tsl(i), :],
                       after=[x_busy[sl]], inc=f"LDX{sl}")
            t_xn = emit_ln_tile(p, sb, sb.x[:, sl, :], tl, prev, d)
            x_busy[sl] = t_xn
            t_ev, t_tr = emit_transpose_tile(
                p, sb, t_xn, d, lambda c, i=i: [sb.xnT[:, c, tsl(i)]],
                psT_busy, ident_tok=t_id)
            prev["trlast"] = t_tr
            tr_last_all = t_ev
        xnT_ready = [tr_last_all]

        wb_busy = [None] * 3
        st_busy = [None, None]
        psP_busy = [None, None]
        projs = [(ap["wqb"], ap["qT"], window, seq),
                 (ap["wkb"], ap["kT"], 0, SP),
                 (ap["wvb"], ap["vT"], 0, SP)]
        blocks = [(pi, ic) for pi in range(3) for ic in range(NCH)]
        wld = {}

        def issue_w(bi):
            pi, ic = blocks[bi]
            sl = bi % 3
            tl = p.dma("gpsimd", sb.wb[sl][:, :, :],
                       projs[pi][0][ic].rearrange("(k p) m -> p k m", p=128),
                       after=[wb_busy[sl]], inc=f"LWU{sl}")
            wld[bi] = tl

        for bi in range(min(3, len(blocks))):
            issue_w(bi)
        for bi, (pi, ic) in enumerate(blocks):
            wsrc, dst, off, length = projs[pi]
            sl = bi % 3
            ng = length // 512
            t_mm = None
            for tg in range(ng):
                par = tg % 2
                pu = sb.psP[par]
                for c in range(NCH):
                    t_mm = p.op("tensor", lambda t, c=c, pu=pu, sl=sl, off=off, tg=tg: t.matmul(
                        out=pu[:, :], lhsT=sb.wb[sl][:, c, :],
                        rhs=sb.xnT[:, c, off + 512 * tg: off + 512 * (tg + 1)],
                        start=(c == 0), stop=(c == NCH - 1)),
                        after=([wld[bi], psP_busy[par]] + xnT_ready) if c == 0 else [],
                        inc="TE" if c == NCH - 1 else None)
                ssl = tg % 2
                t_ev = p.op("vector", lambda v, pu=pu, ssl=ssl: v.tensor_copy(
                    out=sb.st[ssl][:, :], in_=pu[:, :]),
                    after=[t_mm, st_busy[ssl]], inc="VE")
                psP_busy[par] = t_ev
                t_st = p.dma("sync", dst[tsl(ic), 512 * tg: 512 * (tg + 1)],
                             sb.st[ssl][:, :], after=[t_ev], inc=f"STU{ssl}")
                st_busy[ssl] = t_st
            wb_busy[sl] = t_mm
            if bi + 3 < len(blocks):
                issue_w(bi + 3)
        qkv_done = [st_busy[0], st_busy[1]]
        p.emit(block)

    es = contextlib.ExitStack()
    with es:
        sb = SB()
        sb.ident = es.enter_context(nc.sbuf_tensor("s1c_ident", [128, 128], BF16))
        sb.ones = es.enter_context(nc.sbuf_tensor("s1c_ones", [1, 128], BF16))
        sb.n30 = es.enter_context(nc.sbuf_tensor("s1c_n30", [128, 1], F32))
        sb.trf = es.enter_context(nc.sbuf_tensor("s1c_trf", [128, 128], BF16))
        sb.trl = es.enter_context(nc.sbuf_tensor("s1c_trl", [128, 128], BF16))
        sb.npad = es.enter_context(nc.sbuf_tensor("s1c_npad", [128, NTQ], F32))
        sb.attnT = es.enter_context(nc.sbuf_tensor("s1c_attnT", [128, h0, seq], BF16))
        esC = contextlib.ExitStack()
        es.enter_context(esC)
        sb.kh = [esC.enter_context(nc.sbuf_tensor(f"s1c_kh{i}", [128, SP], BF16))
                 for i in range(2)]
        sb.qh = [esC.enter_context(nc.sbuf_tensor(f"s1c_qh{i}", [128, seq], BF16))
                 for i in range(2)]
        sb.vh = esC.enter_context(nc.sbuf_tensor("s1c_vh", [128, SP], BF16))
        sb.vt = esC.enter_context(nc.sbuf_tensor("s1c_vt", [128, NTP, 128], BF16))
        sb.sc = [esC.enter_context(nc.sbuf_tensor(f"s1c_sc{i}", [128, scw], F32))
                 for i in range(3)]
        sb.pr = [esC.enter_context(nc.sbuf_tensor(f"s1c_pr{i}", [128, scw], BF16))
                 for i in range(3)]
        sb.prT = esC.enter_context(nc.sbuf_tensor("s1c_prT", [128, WT, 128], BF16))
        sb.rs = [esC.enter_context(nc.sbuf_tensor(f"s1c_rs{i}", [128, 1], F32))
                 for i in range(3)]
        sb.psS = esC.enter_context(nc.psum_tensor("s1c_psS", [128, ((scw + 511) // 512) * 512], F32))
        sb.psT = [esC.enter_context(nc.psum_tensor(f"s1c_psT{i}", [128, 512], BF16))
                  for i in range(2)]
        sb.psPV = [esC.enter_context(nc.psum_tensor(f"s1c_psPV{i}", [128, 128], F32))
                   for i in range(2)]

        t_id = p.dma("sync", sb.ident[:, :], ap["ident"][:, :], inc="CONST")
        t_tf = p.dma("sync", sb.trf[:, :], ap["tri_first_add"][:, :], inc="CONST")
        t_tl2 = p.dma("sync", sb.trl[:, :], ap["tri_last_add"][:, :], inc="CONST")
        t_np = p.dma("sync", sb.npad[:, :], ap["npadcT"][:, :], inc="CONST")
        t_on = p.op("vector", lambda v: v.memset(sb.ones[:, :], 1.0), inc="VE")
        t_n30 = p.op("vector", lambda v: v.memset(sb.n30[:, :], -30.0), inc="VE")
        consts = [t_id, t_tf, t_tl2, t_np, t_on, t_n30]

        kh_busy = [None, None]
        qh_busy = [None, None]
        sc_busy = [None, None, None]
        vh_busy = None
        vt_busy = None
        kq_loads = {}
        psT_b2 = {}
        psS_busy = {}
        rs_busy = [None, None, None]
        prb_busy = [None, None, None]

        def load_head(h):
            par = h % 2
            t_k = p.dma("gpsimd", sb.kh[par][:, :], ap["kT"][tsl(h), :],
                        after=[kh_busy[par]] + qkv_done, inc=f"KH{par}")
            t_q = p.dma("gpsimd", sb.qh[par][:, :], ap["qT"][tsl(h), :],
                        after=[qh_busy[par]] + qkv_done, inc=f"QH{par}")
            kq_loads[h] = (t_k, t_q)

        load_head(0)
        if h0 > 1:
            load_head(1)
        pv_evac = {0: None, 1: None}
        for h in range(h0):
            par = h % 2
            t_k, t_q = kq_loads[h]
            t_v = p.dma("gpsimd", sb.vh[:, :], ap["vT"][tsl(h), :],
                        after=[vh_busy] + qkv_done, inc="VH")
            tr_toks = []
            for g4 in range((NTP + 3) // 4):
                n_in = min(4, NTP - 4 * g4)
                pt = sb.psT[g4 % 2]
                t_tr = None
                for k4 in range(n_in):
                    kt = 4 * g4 + k4
                    t_tr = p.op("tensor", lambda t, kt=kt, k4=k4, pt=pt: t.transpose(
                        out=pt[:, 128 * k4:128 * (k4 + 1)], in_=sb.vh[:, tsl(kt)],
                        identity=sb.ident[:, :]),
                        after=[t_v, psT_b2.get(g4 % 2), t_id] +
                              ([vt_busy] if kt == 0 else []),
                        inc="TE" if k4 == n_in - 1 else None)
                t_ve = p.op("vector", lambda v, g4=g4, pt=pt, n_in=n_in: v.tensor_copy(
                    out=sb.vt[:, 4 * g4:4 * g4 + n_in, :].rearrange("p a b -> p (a b)"),
                    in_=pt[:, :128 * n_in]), after=[t_tr], inc="VE")
                psT_b2[g4 % 2] = t_ve
                tr_toks.append(t_ve)
            vh_busy = tr_toks[-1]
            vt_ready = tr_toks[-1]
            prT_busy = None

            stage_pipeline = True

            def emit_B1(qt, t_th):
                scb = sb.sc[qt % 3]
                prb = sb.pr[qt % 3]
                rsb = sb.rs[qt % 3]
                t_m1 = p.op("gpsimd", lambda g, scb=scb: g.tensor_tensor(
                    out=scb[:, 0:128], in0=scb[:, 0:128], in1=sb.trf[:, :],
                    op=ALU.add), after=[t_th] + consts, inc="GP")
                t_m2 = p.op("gpsimd", lambda g, scb=scb: g.tensor_tensor(
                    out=scb[:, scw - 128: scw], in0=scb[:, scw - 128: scw],
                    in1=sb.trl[:, :], op=ALU.add), after=[t_m1], inc="GP")
                t_ex = p.op("scalar", lambda s, prb=prb, scb=scb, rsb=rsb, qt=qt: s.activation(
                    out=prb[:, :], in_=scb[:, :], func=AF.Exp, scale=SOFTCAP,
                    bias=sb.n30[:, 0:1], accum_out=rsb[:, :]),
                    after=[t_m2, prb_busy[qt % 3], rs_busy[qt % 3]], inc="SC")
                sc_busy[qt % 3] = t_ex
                t_rc = p.op("gpsimd", lambda g, qt=qt, rsb=rsb: g.tensor_tensor(
                    out=rsb[:, :], in0=rsb[:, :], in1=sb.npad[:, qt:qt + 1],
                    op=ALU.add), after=[t_ex], inc="GP")
                t_rp = p.op("vector", lambda v, rsb=rsb: v.reciprocal(
                    out=rsb[:, :], in_=rsb[:, :]), after=[t_rc], inc="VE")
                rs_busy[qt % 3] = t_rp
                t_pr = p.op("vector", lambda v, prb=prb, rsb=rsb: v.tensor_scalar_mul(
                    out=prb[:, :], in0=prb[:, :], scalar1=rsb[:, 0:1]),
                    after=[t_rp], inc="VE")
                return (qt, t_pr)

            def emit_C1(qt, t_pr):
                nonlocal prT_busy, pv_evac
                prb = sb.pr[qt % 3]
                pt_toks = []
                for g4 in range((WT + 3) // 4):
                    n_in = min(4, WT - 4 * g4)
                    pt = sb.psT[g4 % 2]
                    t_tr = None
                    for k4 in range(n_in):
                        kt = 4 * g4 + k4
                        t_tr = p.op("tensor", lambda t, kt=kt, k4=k4, pt=pt, prb=prb: t.transpose(
                            out=pt[:, 128 * k4: 128 * (k4 + 1)], in_=prb[:, tsl(kt)],
                            identity=sb.ident[:, :]),
                            after=[t_pr, psT_b2.get(g4 % 2)] +
                                  ([prT_busy] if kt == 0 else []),
                            inc="TE" if k4 == n_in - 1 else None)
                    t_pe = p.op("vector", lambda v, g4=g4, pt=pt, n_in=n_in: v.tensor_copy(
                        out=sb.prT[:, 4 * g4: 4 * g4 + n_in, :].rearrange("p a b -> p (a b)"),
                        in_=pt[:, :128 * n_in]), after=[t_tr], inc="VE")
                    psT_b2[g4 % 2] = t_pe
                    pt_toks.append(t_pe)
                t_pv = None
                ppv = sb.psPV[qt % 2]
                for kt in range(WT):
                    t_pv = p.op("tensor", lambda t, kt=kt, qt=qt, ppv=ppv: t.matmul(
                        out=ppv[:, :], lhsT=sb.vt[:, qt + kt, :],
                        rhs=sb.prT[:, kt, :], start=(kt == 0), stop=(kt == WT - 1)),
                        after=(pt_toks + [pv_evac[qt % 2]]) if kt == 0 else [],
                        inc="TE" if kt == WT - 1 else None)
                prT_busy = t_pv
                t_out = p.op("vector", lambda v, h=h, qt=qt, ppv=ppv: v.tensor_copy(
                    out=sb.attnT[:, h, tsl(qt)], in_=ppv[:, :]),
                    after=[t_pv], inc="VE")
                pv_evac[qt % 2] = t_out
                return pt_toks[-1]

            pipeB = {}
            pipeC = {}
            for qt in range(NTQ + 4):
                mm_toks = []
                if qt < NTQ:
                    for kg in range(NKG):
                        width = min(512, scw - 512 * kg)
                        t_mm = p.op("tensor", lambda t, kg=kg, qt=qt, width=width, par=par: t.matmul(
                            out=sb.psS[:, 512 * kg: 512 * kg + width],
                            lhsT=sb.qh[par][:, tsl(qt)],
                            rhs=sb.kh[par][:, 128 * qt + 512 * kg: 128 * qt + 512 * kg + width],
                            start=True, stop=True),
                            after=[psS_busy.get(kg)] +
                                  ([t_k, t_q, vt_ready] if kg == 0 else []),
                            inc="TE")
                        mm_toks.append(t_mm)
                if qt - 4 >= 0 and (qt - 4) in pipeC:
                    cq, cpr = pipeC.pop(qt - 4)
                    last_pe = emit_C1(cq, cpr)
                    prb_busy[cq % 3] = last_pe
                if qt - 2 >= 0 and (qt - 2) in pipeB:
                    t_th2 = pipeB.pop(qt - 2)
                    pipeC[qt - 2] = emit_B1(qt - 2, t_th2)
                if qt < NTQ:
                    t_th = None
                    for kg in range(NKG):
                        width = min(512, scw - 512 * kg)
                        t_th = p.op("scalar", lambda s, qt=qt, kg=kg, width=width: s.activation(
                            out=sb.sc[qt % 3][:, 512 * kg: 512 * kg + width],
                            in_=sb.psS[:, 512 * kg: 512 * kg + width],
                            func=AF.Tanh, scale=SCALE0 / SOFTCAP),
                            after=[mm_toks[kg]] +
                                  ([sc_busy[qt % 3]] if kg == 0 else []),
                            inc="SC")
                        psS_busy[kg] = t_th
                    pipeB[qt] = t_th
            kh_busy[par] = prT_busy
            qh_busy[par] = prT_busy
            if h + 2 < h0:
                load_head(h + 2)
        attn_done = [pv_evac[0], pv_evac[1]]
        p.emit(block)
        esC.close()

        NDH = 2 if d >= 2048 else 1
        dh = d // NDH
        sb.wo = es.enter_context(nc.sbuf_tensor("s1d_wo", [128, NCH, dh], BF16))
        sb.xr = [es.enter_context(nc.sbuf_tensor(f"s1d_xr{i}", [128, dh], F32))
                 for i in range(2)]
        sb.x1t = [es.enter_context(nc.sbuf_tensor(f"s1d_x1t{i}", [128, dh], F32))
                  for i in range(2)]
        sb.psO = [es.enter_context(nc.psum_tensor(f"s1d_psO{i}", [128, 512], F32))
                  for i in range(2)]
        st_last = None
        wo_busy = None
        psO_busy = [None, None]
        xr_busy = [None, None]
        x1_busy = [None, None]
        for half in range(NDH):
            wo_lds = []
            for c in range(NCH):
                tl = p.dma("gpsimd", sb.wo[:, c, :],
                           ap["woT"][tsl(c), dh * half: dh * (half + 1)],
                           after=[wo_busy] + (attn_done if (half == 0 and c < 4) else []),
                           inc=f"LWU{c % 4}", chain=True)
                wo_lds.append(tl)
            wo_ready = wo_lds[-4:]
            for i in range(NTQ):
                sl = i % 2
                t_x0 = p.dma("sync", sb.xr[sl][:, :],
                             ap["x0"][tsl(i + window // 128), dh * half: dh * (half + 1)],
                             after=[xr_busy[sl]] + (attn_done if (half == 0 and i < 2) else []),
                             inc=f"LDX{sl}")
                for dg in range(dh // 512):
                    t_mm = None
                    for c in range(NCH):
                        t_mm = p.op("tensor", lambda t, i=i, c=c, dg=dg: t.matmul(
                            out=sb.psO[dg % 2][:, :], lhsT=sb.attnT[:, c, tsl(i)],
                            rhs=sb.wo[:, c, 512 * dg: 512 * (dg + 1)],
                            start=(c == 0), stop=(c == NCH - 1)),
                            after=(wo_ready + [psO_busy[dg % 2]] + attn_done)
                            if c == 0 else [],
                            inc="TE" if c == NCH - 1 else None)
                    t_add = p.op("vector", lambda v, sl=sl, dg=dg: v.tensor_add(
                        out=sb.x1t[sl][:, 512 * dg: 512 * (dg + 1)],
                        in0=sb.psO[dg % 2][:, :],
                        in1=sb.xr[sl][:, 512 * dg: 512 * (dg + 1)]),
                        after=[t_mm, t_x0, x1_busy[sl]], inc="VE")
                    psO_busy[dg % 2] = t_add
                xr_busy[sl] = psO_busy[(dh // 512 - 1) % 2]
                t_st = p.dma("sync", ap["x1"][tsl(i), dh * half: dh * (half + 1)],
                             sb.x1t[sl][:, :], after=[psO_busy[0], psO_busy[1]],
                             inc=f"STU{sl}")
                x1_busy[sl] = t_st
                st_last = t_st
            wo_busy = t_mm
        p.emit(block)
    return [st_last]



def emit_s3(p, block, ap, seq=S, upto='D'):
    nc = p.nc
    NT = seq // 128
    NTG = seq // 512
    OWN = seq // 2
    NTJ = OWN // 128
    NOG = OWN // 512
    NCH = D // 128

    es = contextlib.ExitStack()
    with es:
        sb = SB()
        sb.ident = es.enter_context(nc.sbuf_tensor("s3_ident", [128, 128], BF16))
        sb.x = es.enter_context(nc.sbuf_tensor("s3_x", [128, 2, D], F32))
        sb.stats6 = es.enter_context(nc.sbuf_tensor("s3_st6", [128, 24], F32))
        sb.mv = es.enter_context(nc.sbuf_tensor("s3_mv", [128, 2], F32))
        sb.veps = es.enter_context(nc.sbuf_tensor("s3_veps", [128, 1], F32))
        sb.rstd = es.enter_context(nc.sbuf_tensor("s3_rstd", [128, 1], F32))
        sb.xn = es.enter_context(nc.sbuf_tensor("s3_xn", [128, D], BF16))
        sb.xg = [es.enter_context(nc.sbuf_tensor(f"s3_xg{i}", [128, NCH, 512], BF16))
                 for i in range(2)]
        sb.psT = [es.enter_context(nc.psum_tensor(f"s3_psT{i}", [128, 128], BF16))
                  for i in range(2)]
        t_id = p.dma("sync", sb.ident[:, :], ap["ident"][:, :], inc="CONST")
        psT_busy = [None, None]
        x_busy = [None, None]
        xg_busy = [None, None]
        prev = {}
        for g in range(NTG):
            gsl = g % 2
            evs = []
            for ii in range(4):
                i = 4 * g + ii
                sl = i % 2
                tl = p.dma("sync", sb.x[:, sl, :], ap["x2"][tsl(i), :],
                           after=[x_busy[sl]], inc=f"LDX{sl}")
                t_xn = emit_ln_tile(p, sb, sb.x[:, sl, :], tl, prev, D)
                x_busy[sl] = t_xn
                t_ev, t_tr = emit_transpose_tile(
                    p, sb, t_xn, D,
                    lambda c, ii=ii, gsl=gsl: [sb.xg[gsl][:, c, tsl(ii)]],
                    psT_busy, ident_tok=t_id,
                    extra0=(xg_busy[gsl] or []) if ii == 0 else ())
                prev["trlast"] = t_tr
                evs.append(t_ev)
            sts = []
            for c in range(NCH):
                sts.append(p.dma(
                    "sync", ap["xnT_d"][tsl(c), 512 * g: 512 * (g + 1)],
                    sb.xg[gsl][:, c, :],
                    after=(evs + list(xg_busy[gsl] or [])) if c < 2 else [],
                    inc=f"STU{2 * gsl + c % 2}"))
            xg_busy[gsl] = sts[-2:]
        xnT_done = list((xg_busy[0] or [])) + list((xg_busy[1] or []))
        p.emit(block)
    if upto == 'A':
        return xnT_done

    es = contextlib.ExitStack()
    with es:
        sb = SB()
        sb.onesc = es.enter_context(nc.sbuf_tensor("s3_onesc", [128, 1], F32))
        sb.onesr = es.enter_context(nc.sbuf_tensor("s3_onesr", [1, 128], F32))
        sb.epsr = es.enter_context(nc.sbuf_tensor("s3_epsr", [1, 1], F32))
        sb.wkva = es.enter_context(nc.sbuf_tensor("s3_wkva", [128, NCH, 576], BF16))
        sb.gkv = es.enter_context(nc.sbuf_tensor("s3_gkv", [128, 4], F32))
        sb.xr = [es.enter_context(nc.sbuf_tensor(f"s3_xr{i}", [128, 512], BF16))
                 for i in range(2 * NCH)]
        sb.sq = es.enter_context(nc.sbuf_tensor("s3_sq", [128, 512], F32))
        sb.ckvf = es.enter_context(nc.sbuf_tensor("s3_ckvf", [128, 4, 512], F32))
        sb.krf = es.enter_context(nc.sbuf_tensor("s3_krf", [64, 512], F32))
        sb.kr1 = es.enter_context(nc.sbuf_tensor("s3_kr1", [32, 512], F32))
        sb.kr2 = es.enter_context(nc.sbuf_tensor("s3_kr2", [32, 512], F32))
        sb.rstd = es.enter_context(nc.sbuf_tensor("s3_rstdr", [1, 512], F32))
        sb.ckvn = es.enter_context(nc.sbuf_tensor("s3_ckvn", [128, 4, seq], BF16))
        sb.krot = es.enter_context(nc.sbuf_tensor("s3_krot", [64, seq], BF16))
        sb.cosk = es.enter_context(nc.sbuf_tensor("s3_cosk", [64, seq], F32))
        sb.sink = es.enter_context(nc.sbuf_tensor("s3_sink", [64, seq], F32))
        sb.psKV = [es.enter_context(nc.psum_tensor(f"s3_psKV{i}", [128, 512], F32))
                   for i in range(5)]
        sb.psST = es.enter_context(nc.psum_tensor("s3_psST", [1, 512], F32))
        sb.psB = es.enter_context(nc.psum_tensor("s3_psB", [128, 512], F32))

        t_w = p.dma("gpsimd", sb.wkva[:, :, :],
                    ap["wkv_a"].rearrange("(k p) m -> p k m", p=128),
                    after=xnT_done, inc="LWW")
        t_g = p.dma("sync", sb.gkv[:, :], ap["gkv_2d"][:, :], after=xnT_done,
                    inc="CONST")
        t_ck = p.dma("sync", sb.cosk[:, :], ap["cos_k"][:, :], after=xnT_done,
                     inc="CONST")
        t_sk = p.dma("sync", sb.sink[:, :], ap["sin_k"][:, :], after=xnT_done,
                     inc="CONST")
        t_oc = p.op("vector", lambda v: v.memset(sb.onesc[:, :], 1.0), inc="VE")
        t_or = p.op("vector", lambda v: v.memset(sb.onesr[:, :], 1.0), inc="VE")
        t_ep = p.op("vector", lambda v: v.memset(sb.epsr[:, :], float(EPS_RMS)), inc="VE")
        consts = [t_w, t_g, t_ck, t_sk, t_oc, t_or, t_ep]

        xr_busy = [None] * (2 * NCH)
        psKV_busy = [None] * 5
        sq_busy = None
        ckvf_busy = [None] * 4
        krf_busy = None
        kr_busy = None
        psST_busy = None
        psB_busy = None
        rstd_busy = None
        xr_lds = {}

        def ld_xr(g, c):
            sl = (g % 2) * NCH + c
            tl = p.dma("gpsimd", sb.xr[sl][:, :],
                       ap["xnT_d"][tsl(c), 512 * g: 512 * (g + 1)],
                       after=[xr_busy[sl]] +
                             (xnT_done if (g == 0 and c < 4) else []),
                       inc=f"LWU{c % 4}")
            xr_lds[(g, c)] = tl

        for c0 in range(NCH):
            ld_xr(0, c0)
        for g in range(NTG):
            if g + 1 < NTG:
                for c0 in range(NCH):
                    ld_xr(g + 1, c0)
            mm_last = [None] * 5
            for jc in range(5):
                w_off = 128 * jc
                w_w = 128 if jc < 4 else 64
                t_mm = None
                for c in range(NCH):
                    sl = (g % 2) * NCH + c
                    t_mm = p.op("tensor", lambda t, c=c, jc=jc, sl=sl, w_off=w_off, w_w=w_w, g=g: t.matmul(
                        out=sb.psKV[jc][:w_w if jc == 4 else 128, :],
                        lhsT=sb.wkva[:, c, w_off:w_off + w_w],
                        rhs=sb.xr[sl][:, :],
                        start=(c == 0), stop=(c == NCH - 1)),
                        after=[xr_lds[(g, c)], consts[0]] +
                              ([psKV_busy[jc]] if c == 0 else []),
                        inc="TE" if c == NCH - 1 else None)
                mm_last[jc] = t_mm
            for c in range(NCH):
                xr_busy[(g % 2) * NCH + c] = mm_last[4]
            sq_toks = []
            st_mm = None
            for jc in range(4):
                t_cp = p.op("vector", lambda v, jc=jc: v.tensor_copy(
                    out=sb.ckvf[:, jc, :], in_=sb.psKV[jc][:, :]),
                    after=[mm_last[jc], ckvf_busy[jc]], inc="VE")
                psKV_busy[jc] = t_cp
                sq_toks.append(t_cp)
            for jc in range(4):
                t_sq = p.op("scalar", lambda s, jc=jc: s.activation(
                    out=sb.sq[:, :], in_=sb.ckvf[:, jc, :], func=AF.Square),
                    after=[sq_toks[jc], sq_busy], inc="SC")
                st_mm = p.op("tensor", lambda t, jc=jc: t.matmul(
                    out=sb.psST[:, :], lhsT=sb.onesc[:, :], rhs=sb.sq[:, :],
                    start=(jc == 0), stop=(jc == 3)),
                    after=[t_sq, t_oc] + ([psST_busy] if jc == 0 else []),
                    inc="TE")
                sq_busy = st_mm
            t_kr = p.op("vector", lambda v: v.tensor_copy(
                out=sb.krf[:, :], in_=sb.psKV[4][:64, :]),
                after=[mm_last[4], krf_busy], inc="VE")
            psKV_busy[4] = t_kr
            t_ln = p.op("scalar", lambda s: s.activation(
                out=sb.rstd[:, :], in_=sb.psST[:, :], func=AF.Ln,
                scale=1.0 / KVLR, bias=sb.epsr[:, 0:1]),
                after=[st_mm, rstd_busy, t_ep], inc="SC")
            psST_busy = t_ln
            t_ex = p.op("scalar", lambda s: s.activation(
                out=sb.rstd[:, :], in_=sb.rstd[:, :], func=AF.Exp, scale=-0.5),
                after=[t_ln], inc="SC")
            t_bc = p.op("tensor", lambda t: t.matmul(
                out=sb.psB[:, :], lhsT=sb.onesr[:, :], rhs=sb.rstd[:, :],
                start=True, stop=True), after=[t_ex, t_or, psB_busy], inc="TE")
            t_n_last = None
            for jc in range(4):
                t_n1 = p.op("vector", lambda v, jc=jc: v.tensor_tensor(
                    out=sb.ckvf[:, jc, :], in0=sb.ckvf[:, jc, :], in1=sb.psB[:, :],
                    op=ALU.mult), after=[t_bc, sq_toks[jc]], inc="VE")
                t_n2 = p.op("vector", lambda v, jc=jc, g=g: v.tensor_scalar_mul(
                    out=sb.ckvn[:, jc, 512 * g: 512 * (g + 1)],
                    in0=sb.ckvf[:, jc, :], scalar1=sb.gkv[:, jc:jc + 1]),
                    after=[t_n1, t_g], inc="VE")
                ckvf_busy[jc] = t_n2
                t_n_last = t_n2
            psB_busy = t_n_last
            rstd_busy = t_bc
            cs = (512 * g, 512 * (g + 1))
            t_r1 = p.op("vector", lambda v, cs=cs: v.tensor_mul(
                out=sb.kr1[:, :], in0=sb.krf[0:32, :], in1=sb.cosk[0:32, cs[0]:cs[1]]),
                after=[t_kr, t_ck, kr_busy], inc="VE")
            t_r2 = p.op("vector", lambda v, cs=cs: v.tensor_mul(
                out=sb.kr2[:, :], in0=sb.krf[32:64, :], in1=sb.sink[32:64, cs[0]:cs[1]]),
                after=[t_kr, t_sk], inc="VE")
            t_r3 = p.op("vector", lambda v, cs=cs: v.tensor_sub(
                out=sb.kr1[:, :], in0=sb.kr1[:, :], in1=sb.kr2[:, :]),
                after=[t_r1, t_r2], inc="VE")
            t_r4 = p.op("vector", lambda v, cs=cs: v.tensor_copy(
                out=sb.krot[0:32, cs[0]:cs[1]], in_=sb.kr1[:, :]),
                after=[t_r3], inc="VE")
            t_r5 = p.op("vector", lambda v, cs=cs: v.tensor_mul(
                out=sb.kr1[:, :], in0=sb.krf[0:32, :], in1=sb.sink[0:32, cs[0]:cs[1]]),
                after=[t_r4], inc="VE")
            t_r6 = p.op("vector", lambda v, cs=cs: v.tensor_mul(
                out=sb.kr2[:, :], in0=sb.krf[32:64, :], in1=sb.cosk[32:64, cs[0]:cs[1]]),
                after=[t_r3], inc="VE")
            t_r7 = p.op("vector", lambda v, cs=cs: v.tensor_add(
                out=sb.kr1[:, :], in0=sb.kr1[:, :], in1=sb.kr2[:, :]),
                after=[t_r5, t_r6], inc="VE")
            t_r8 = p.op("vector", lambda v, cs=cs: v.tensor_copy(
                out=sb.krot[32:64, cs[0]:cs[1]], in_=sb.kr1[:, :]),
                after=[t_r7], inc="VE")
            krf_busy = t_r7
            kr_busy = t_r8
        t_krst = p.dma("sync", ap["krot_d"][:, :], sb.krot[:, :],
                       after=[kr_busy, t_r4], inc="STU0")
        kv_done = [t_n_last, t_krst]
        if upto == 'B1':
            p.emit(block)
            return kv_done
        sb.wkvb = es.enter_context(nc.sbuf_tensor("s3_wkvb", [128, 4, H1 * DN], BF16))
        sb.wkvv = es.enter_context(nc.sbuf_tensor("s3_wkvv", [128, 4, H1 * DV], BF16))
        sb.stg = [es.enter_context(nc.sbuf_tensor(f"s3_stg{i}", [128, 512], BF16))
                  for i in range(2)]
        t_wk = p.dma("gpsimd", sb.wkvb[:, :, :],
                     ap["wkv_b_k"].rearrange("(k p) m -> p k m", p=128), inc="LWW")
        t_wv = p.dma("gpsimd", sb.wkvv[:, :, :],
                     ap["wkv_b_v"].rearrange("(k p) m -> p k m", p=128), inc="LWW")
        stg_busy = [None, None]
        psKN_busy = [None, None]
        for out_d, t_wx, which in ((ap["knope_d"], t_wk, "k"), (ap["vT_d"], t_wv, "v")):
            wsb = sb.wkvb if which == "k" else sb.wkvv
            for hb in range(H1):
                for g in range(NTG):
                    par = g % 2
                    t_mm = None
                    for l in range(4):
                        t_mm = p.op("tensor", lambda t, hb=hb, g=g, l=l, par=par, wsb=wsb: t.matmul(
                            out=sb.psKV[par][:, :], lhsT=wsb[:, l, tsl(hb)],
                            rhs=sb.ckvn[:, l, 512 * g: 512 * (g + 1)],
                            start=(l == 0), stop=(l == 3)),
                            after=([t_wx, psKN_busy[par]] + kv_done) if l == 0 else [],
                            inc="TE" if l == 3 else None)
                    ssl = par
                    t_ev = p.op("vector", lambda v, ssl=ssl, par=par: v.tensor_copy(
                        out=sb.stg[ssl][:, :], in_=sb.psKV[par][:, :]),
                        after=[t_mm, stg_busy[ssl]], inc="VE")
                    psKN_busy[par] = t_ev
                    t_st = p.dma("sync", out_d[tsl(hb), 512 * g: 512 * (g + 1)],
                                 sb.stg[ssl][:, :], after=[t_ev], inc=f"STU{ssl}")
                    stg_busy[ssl] = t_st
        kvb_done = [stg_busy[0], stg_busy[1]]
        p.emit(block)
        if upto == 'B':
            return kvb_done

    es = contextlib.ExitStack()
    with es:
        sb = SB()
        sb.onesc = es.enter_context(nc.sbuf_tensor("s3q_onesc", [128, 1], F32))
        sb.onesr = es.enter_context(nc.sbuf_tensor("s3q_onesr", [1, 128], F32))
        sb.epsr = es.enter_context(nc.sbuf_tensor("s3q_epsr", [1, 1], F32))
        sb.rstd = es.enter_context(nc.sbuf_tensor("s3q_rstd", [1, 512], F32))
        sb.qlrn = es.enter_context(nc.sbuf_tensor("s3q_qlrn", [128, 12, OWN], BF16))
        esq1 = contextlib.ExitStack()
        es.enter_context(esq1)
        sb.gidx = esq1.enter_context(nc.sbuf_tensor("s3q_gidx", [1, NTJ], mybir.dt.int32))
        sb.xno = esq1.enter_context(nc.sbuf_tensor("s3q_xno", [128, NCH, OWN], BF16))
        sb.wqa = esq1.enter_context(nc.sbuf_tensor("s3q_wqa", [128, NCH, QLR], BF16))
        sb.gq = esq1.enter_context(nc.sbuf_tensor("s3q_gq", [128, QLR // 128], F32))
        sb.sq = esq1.enter_context(nc.sbuf_tensor("s3q_sq", [128, 512], F32))
        sb.qlrf = esq1.enter_context(nc.sbuf_tensor("s3q_qlrf", [128, 12, 512], F32))
        sb.psQ = [esq1.enter_context(nc.psum_tensor(f"s3q_psQ{i}", [128, 512], F32))
                  for i in range(4)]
        sb.psST = esq1.enter_context(nc.psum_tensor("s3q_psST", [1, 512], F32))
        sb.psB = esq1.enter_context(nc.psum_tensor("s3q_psB", [128, 512], F32))

        t_oc = p.op("vector", lambda v: v.memset(sb.onesc[:, :], 1.0), inc="VE")
        t_or = p.op("vector", lambda v: v.memset(sb.onesr[:, :], 1.0), inc="VE")
        t_ep = p.op("vector", lambda v: v.memset(sb.epsr[:, :], float(EPS_RMS)), inc="VE")
        t_gi = p.dma("sync", sb.gidx[:, :], ap["gidx_xnT"][:, :],
                     after=kvb_done, inc="CONST")
        t_wqa = p.dma("gpsimd", sb.wqa[:, :, :],
                      ap["wq_a"].rearrange("(k p) m -> p k m", p=128),
                      after=kvb_done, inc="LWW")
        t_gq = p.dma("sync", sb.gq[:, :], ap["gq_2d"][:, :], after=kvb_done,
                     inc="CONST")

        xnTd_h = ap["xnT_d_h"]

        def _emit_gather(q):
            last = None
            with q.register("gthr") as gr:
                for j in range(NTJ):
                    q.reg_load(gr, sb.gidx[0:1, j:j + 1])
                    last = q.dma_start(
                        out=sb.xno[:, :, tsl(j)],
                        in_=bass.AP(xnTd_h, gr,
                                    [[seq, 128], [128 * seq, NCH], [1, 128]]))
                    if j < NTJ - 1:
                        last.then_inc(p.semh["GTH"], 16)
            return last
        p.sem("GTH")
        p.cnt["GTH"] += 16 * NTJ
        t_gth = ("GTH", p.cnt["GTH"], None)
        p.last_dma["GTH"] = t_gth
        p.recs["gpsimd"].append((_emit_gather, tuple([t_gi] + kvb_done),
                                 ("GTH", 16)))

        qlrn_done = []
        psQ_busy = [None] * 4
        psST_busy = None
        psB_busy = None
        rstd_busy = None
        sq_busy = None
        qlrf_busy = [None] * 12
        for og in range(NOG):
            for lg in range(3):
                t_mm4 = [None] * 4
                for l4 in range(4):
                    l = 4 * lg + l4
                    t_mm = None
                    for c in range(NCH):
                        t_mm = p.op("tensor", lambda t, c=c, l=l, l4=l4, og=og: t.matmul(
                            out=sb.psQ[l4][:, :], lhsT=sb.wqa[:, c, tsl(l)],
                            rhs=sb.xno[:, c, 512 * og: 512 * (og + 1)],
                            start=(c == 0), stop=(c == NCH - 1)),
                            after=([t_wqa, t_gth, psQ_busy[l4]]) if c == 0 else [],
                            inc="TE" if c == NCH - 1 else None)
                    t_mm4[l4] = t_mm
                for l4 in range(4):
                    l = 4 * lg + l4
                    t_cp = p.op("vector", lambda v, l=l, l4=l4: v.tensor_copy(
                        out=sb.qlrf[:, l, :], in_=sb.psQ[l4][:, :]),
                        after=[t_mm4[l4], qlrf_busy[l]], inc="VE")
                    psQ_busy[l4] = t_cp
                    qlrf_busy[l] = t_cp
            for l in range(12):
                t_sq = p.op("scalar", lambda s, l=l: s.activation(
                    out=sb.sq[:, :], in_=sb.qlrf[:, l, :], func=AF.Square),
                    after=[qlrf_busy[l], sq_busy], inc="SC")
                st_mm = p.op("tensor", lambda t, l=l: t.matmul(
                    out=sb.psST[:, :], lhsT=sb.onesc[:, :], rhs=sb.sq[:, :],
                    start=(l == 0), stop=(l == 11)),
                    after=[t_sq, t_oc] + ([psST_busy] if l == 0 else []),
                    inc="TE")
                sq_busy = st_mm
            t_ln = p.op("scalar", lambda s: s.activation(
                out=sb.rstd[:, :], in_=sb.psST[:, :], func=AF.Ln,
                scale=1.0 / QLR, bias=sb.epsr[:, 0:1]),
                after=[sq_busy, rstd_busy, t_ep], inc="SC")
            psST_busy = t_ln
            t_exr = p.op("scalar", lambda s: s.activation(
                out=sb.rstd[:, :], in_=sb.rstd[:, :], func=AF.Exp, scale=-0.5),
                after=[t_ln], inc="SC")
            t_bc = p.op("tensor", lambda t: t.matmul(
                out=sb.psB[:, :], lhsT=sb.onesr[:, :], rhs=sb.rstd[:, :],
                start=True, stop=True), after=[t_exr, t_or, psB_busy], inc="TE")
            rstd_busy = t_bc
            t_l2 = None
            for l in range(12):
                t_n1 = p.op("vector", lambda v, l=l: v.tensor_tensor(
                    out=sb.qlrf[:, l, :], in0=sb.qlrf[:, l, :], in1=sb.psB[:, :],
                    op=ALU.mult), after=[t_bc, qlrf_busy[l]], inc="VE")
                t_l2 = p.op("vector", lambda v, l=l, og=og: v.tensor_scalar_mul(
                    out=sb.qlrn[:, l, 512 * og: 512 * (og + 1)],
                    in0=sb.qlrf[:, l, :], scalar1=sb.gq[:, l:l + 1]),
                    after=[t_n1, t_gq], inc="VE")
                qlrf_busy[l] = t_l2
            psB_busy = t_l2
            qlrn_done.append(t_l2)
        p.emit(block)
        esq1.close()

        es2 = contextlib.ExitStack()
        with es2:
            sb2 = SB()
            sb2.wqb = es2.enter_context(nc.sbuf_tensor("s3q_wqb", [128, 12, 3072], BF16))
            sb2.stg = [es2.enter_context(nc.sbuf_tensor(f"s3q_stg{i}", [128, 512], BF16))
                       for i in range(2)]
            sb2.rpf = es2.enter_context(nc.sbuf_tensor("s3q_rpf", [128, 512], F32))
            sb2.rp1 = es2.enter_context(nc.sbuf_tensor("s3q_rp1", [128, 512], F32))
            sb2.rp2 = es2.enter_context(nc.sbuf_tensor("s3q_rp2", [128, 512], F32))
            sb2.cosq = es2.enter_context(nc.sbuf_tensor("s3q_cosq", [128, OWN], F32))
            sb2.sinq = es2.enter_context(nc.sbuf_tensor("s3q_sinq", [128, OWN], F32))
            sb2.psQ = es2.enter_context(nc.psum_tensor("s3q2_psQ", [128, 512], F32))
            t_wqb = p.dma("gpsimd", sb2.wqb[:, :, :],
                          ap["wq_b"].rearrange("(k p) m -> p k m", p=128),
                          after=[t_wqa, sq_busy] + qlrn_done, inc="LWW")
            t_cq = p.dma("sync", sb2.cosq[:, :], ap["cos_q"][:, :],
                         after=[sq_busy] + qlrn_done, inc="CONST")
            t_sq2 = p.dma("sync", sb2.sinq[:, :], ap["sin_q"][:, :],
                          after=[sq_busy] + qlrn_done, inc="CONST")
            psQ_busy2 = None
            stg_busy = [None, None]
            rp_busy = None
            for og in range(NOG):
                for ib in range(24):
                    t_mm = None
                    for l in range(12):
                        t_mm = p.op("tensor", lambda t, ib=ib, l=l, og=og: t.matmul(
                            out=sb2.psQ[:, :], lhsT=sb2.wqb[:, l, tsl(ib)],
                            rhs=sb.qlrn[:, l, 512 * og: 512 * (og + 1)],
                            start=(l == 0), stop=(l == 11)),
                            after=([t_wqb, psQ_busy2] + qlrn_done) if l == 0 else [],
                            inc="TE" if l == 11 else None)
                    ssl = ib % 2
                    if ib < 16:
                        t_ev = p.op("vector", lambda v, ssl=ssl: v.tensor_copy(
                            out=sb2.stg[ssl][:, :], in_=sb2.psQ[:, :]),
                            after=[t_mm, stg_busy[ssl]], inc="VE")
                        psQ_busy2 = t_ev
                    else:
                        cs = (512 * og, 512 * (og + 1))
                        t_f = p.op("vector", lambda v: v.tensor_copy(
                            out=sb2.rpf[:, :], in_=sb2.psQ[:, :]),
                            after=[t_mm, rp_busy], inc="VE")
                        psQ_busy2 = t_f
                        lastc = None
                        for half in range(2):
                            b0 = 64 * half
                            x1 = sb2.rpf[b0:b0 + 32, :]
                            x2 = sb2.rpf[b0 + 32:b0 + 64, :]
                            ta = p.op("vector", lambda v, x1=x1, cs=cs, b0=b0: v.tensor_mul(
                                out=sb2.rp1[b0:b0 + 32, :], in0=x1,
                                in1=sb2.cosq[b0:b0 + 32, cs[0]:cs[1]]), after=[t_f, t_cq, rp_busy], inc="VE")
                            tb = p.op("vector", lambda v, x2=x2, cs=cs, b0=b0: v.tensor_mul(
                                out=sb2.rp2[b0:b0 + 32, :], in0=x2,
                                in1=sb2.sinq[b0 + 32:b0 + 64, cs[0]:cs[1]]), after=[t_f, t_sq2], inc="VE")
                            tc = p.op("vector", lambda v, b0=b0: v.tensor_sub(
                                out=sb2.rp1[b0:b0 + 32, :], in0=sb2.rp1[b0:b0 + 32, :],
                                in1=sb2.rp2[b0:b0 + 32, :]), after=[ta, tb], inc="VE")
                            td = p.op("vector", lambda v, x1=x1, cs=cs, b0=b0: v.tensor_mul(
                                out=sb2.rp1[b0 + 32:b0 + 64, :], in0=x1,
                                in1=sb2.sinq[b0:b0 + 32, cs[0]:cs[1]]), after=[ta], inc="VE")
                            te = p.op("vector", lambda v, x2=x2, cs=cs, b0=b0: v.tensor_mul(
                                out=sb2.rp2[b0 + 32:b0 + 64, :], in0=x2,
                                in1=sb2.cosq[b0 + 32:b0 + 64, cs[0]:cs[1]]), after=[tb], inc="VE")
                            tf = p.op("vector", lambda v, b0=b0: v.tensor_add(
                                out=sb2.rp1[b0 + 32:b0 + 64, :],
                                in0=sb2.rp1[b0 + 32:b0 + 64, :],
                                in1=sb2.rp2[b0 + 32:b0 + 64, :]), after=[td, te], inc="VE")
                            lastc = tf
                        ssl = ib % 2
                        t_ev = p.op("vector", lambda v, ssl=ssl: v.tensor_copy(
                            out=sb2.stg[ssl][:, :], in_=sb2.rp1[:, :]),
                            after=[lastc, stg_busy[ssl]], inc="VE")
                        rp_busy = t_ev
                    t_st = p.dma("sync", ap["qT_d"][tsl(ib), 512 * og: 512 * (og + 1)],
                                 sb2.stg[ssl][:, :], after=[t_ev], inc=f"STU{ssl}")
                    stg_busy[ssl] = t_st
            q_done = [stg_busy[0], stg_busy[1]]
            p.emit(block)
    if upto == 'Q':
        return q_done

    es = contextlib.ExitStack()
    with es:
        sb = SB()
        sb.ident = es.enter_context(nc.sbuf_tensor("s3c_ident", [128, 128], BF16))
        sb.n16 = es.enter_context(nc.sbuf_tensor("s3c_n16", [128, 1], F32))
        sb.mask2 = es.enter_context(nc.sbuf_tensor("s3c_mask2", [128, NTJ, 256], BF16))
        sb.krot = es.enter_context(nc.sbuf_tensor("s3c_krot", [64, seq], BF16))
        sb.attnT = es.enter_context(nc.sbuf_tensor("s3c_attnT", [128, H1, OWN], BF16))
        esC = contextlib.ExitStack()
        es.enter_context(esC)
        sb.kh = [esC.enter_context(nc.sbuf_tensor(f"s3c_kh{i}", [128, seq], BF16))
                 for i in range(2)]
        sb.qn = [esC.enter_context(nc.sbuf_tensor(f"s3c_qn{i}", [128, OWN], BF16))
                 for i in range(2)]
        sb.qr = [esC.enter_context(nc.sbuf_tensor(f"s3c_qr{i}", [64, OWN], BF16))
                 for i in range(2)]
        sb.vh = esC.enter_context(nc.sbuf_tensor("s3c_vh", [128, seq], BF16))
        sb.vt = esC.enter_context(nc.sbuf_tensor("s3c_vt", [128, NT, 128], BF16))
        sb.scm = [esC.enter_context(nc.sbuf_tensor(f"s3c_scm{i}", [128, 256], F32))
                  for i in range(4)]
        sb.pr = [esC.enter_context(nc.sbuf_tensor(f"s3c_pr{i}", [128, seq], BF16))
                 for i in range(4)]
        sb.prT = esC.enter_context(nc.sbuf_tensor("s3c_prT", [128, NT, 128], BF16))
        sb.rs = [esC.enter_context(nc.sbuf_tensor(f"s3c_rs{i}", [128, 12], F32))
                 for i in range(4)]
        sb.psS = [esC.enter_context(nc.psum_tensor(f"s3c_psS{i}", [128, 512], F32))
                  for i in range(2)]
        sb.psM = [esC.enter_context(nc.psum_tensor(f"s3c_psM{i}", [128, 256], F32))
                  for i in range(2)]
        sb.psT = [esC.enter_context(nc.psum_tensor(f"s3c_psT{i}", [128, 512], BF16))
                  for i in range(2)]
        sb.psPV = [esC.enter_context(nc.psum_tensor(f"s3c_psPV{i}", [128, 128], F32))
                   for i in range(2)]

        t_id = p.dma("sync", sb.ident[:, :], ap["ident"][:, :], after=q_done,
                     inc="CONST")
        t_mk = p.dma("sync", sb.mask2[:, :, :].rearrange("p a b -> p (a b)"),
                     ap["mask2"][:, :], after=q_done, inc="CONST")
        t_kr2 = p.dma("gpsimd", sb.krot[:, :], ap["krot_d"][:, :],
                      after=kv_done + q_done, inc="KH0")
        t_n16 = p.op("vector", lambda v: v.memset(sb.n16[:, :], -16.0), inc="VE")
        consts = [t_id, t_mk, t_kr2, t_n16]

        kh_busy = [None, None]
        qn_busy = [None, None]
        vh_busy = None
        vt_busy = None
        psT_b2 = {}
        psS_busy = {}
        psM_busy = [None, None]
        scm_busy = [None] * 4
        prb_busy = [None] * 4
        rs_busy = [None] * 4
        kq_loads = {}

        def load_head(h):
            par = h % 2
            t_k = p.dma("gpsimd", sb.kh[par][:, :], ap["knope_d"][tsl(h), :],
                        after=[kh_busy[par]] + kvb_done + q_done, inc=f"KH{par}")
            t_q = p.dma("gpsimd", sb.qn[par][:, :], ap["qT_d"][tsl(h), :],
                        after=[qn_busy[par]] + q_done, inc=f"QH{par}")
            t_q2 = p.dma("gpsimd", sb.qr[par][:, :],
                         ap["qT_d"][16 * DN + 64 * h: 16 * DN + 64 * (h + 1), :],
                         after=[qn_busy[par]], inc=f"QH{par}")
            kq_loads[h] = (t_k, t_q, t_q2)

        load_head(0)
        load_head(1)
        pv_evac = {0: None, 1: None}
        for h in range(H1):
            par = h % 2
            t_k, t_q, t_q2 = kq_loads[h]
            t_v = p.dma("gpsimd", sb.vh[:, :], ap["vT_d"][tsl(h), :],
                        after=[vh_busy] + kvb_done + (q_done if h == 0 else []),
                        inc="VH")
            tr_toks = []
            for kt in range(NT):
                pt = sb.psT[kt % 2]
                t_tr = p.op("tensor", lambda t, kt=kt, pt=pt: t.transpose(
                    out=pt[:, 0:128], in_=sb.vh[:, tsl(kt)], identity=sb.ident[:, :]),
                    after=[t_v, psT_b2.get(kt % 2), t_id] + ([vt_busy] if kt == 0 else []),
                    inc="TE")
                t_ve = p.op("vector", lambda v, kt=kt, pt=pt: v.tensor_copy(
                    out=sb.vt[:, kt, :], in_=pt[:, 0:128]), after=[t_tr], inc="VE")
                psT_b2[kt % 2] = t_ve
                tr_toks.append(t_ve)
            vh_busy = tr_toks[-1]
            vt_ready = tr_toks[-1]
            prT_busy = None

            def emit_B(j, toks):
                exp_toks, t_mmM, nsc = toks
                prb = sb.pr[j % 4]
                rsb = sb.rs[j % 4]
                scmb = sb.scm[j % 4]
                t_sm = p.op("vector", lambda v, scmb=scmb, j=j: v.scalar_tensor_tensor(
                    out=scmb[:, :], in0=sb.psM[j % 2][:, :], scalar=SCALE1,
                    in1=sb.mask2[:, j, :], op0=ALU.mult, op1=ALU.add),
                    after=[t_mmM, scm_busy[j % 4], t_mk], inc="VE")
                nmain = 2 * j
                t_exm = p.op("scalar", lambda s, prb=prb, scmb=scmb, rsb=rsb, nsc=nsc, j=j, nmain=nmain: s.activation(
                    out=prb[:, 128 * nmain: 128 * nmain + 256], in_=scmb[:, :],
                    func=AF.Exp, bias=sb.n16[:, 0:1],
                    accum_out=rsb[:, nsc:nsc + 1]),
                    after=[t_sm, prb_busy[j % 4], rs_busy[j % 4]], inc="SC")
                scm_busy[j % 4] = t_exm
                psM_ret = t_sm
                t_red = p.op("vector", lambda v, rsb=rsb, nsc=nsc: v.reduce_sum(
                    out=rsb[:, 11:12], in_=rsb[:, 0:nsc + 1], axis=AX.X),
                    after=[t_exm] + exp_toks, inc="VE")
                t_rp = p.op("vector", lambda v, rsb=rsb: v.reciprocal(
                    out=rsb[:, 11:12], in_=rsb[:, 11:12]), after=[t_red], inc="VE")
                rs_busy[j % 4] = t_rp
                width = (2 * j + 2) * 128
                t_pr = p.op("vector", lambda v, prb=prb, rsb=rsb, width=width: v.tensor_scalar_mul(
                    out=prb[:, :width], in0=prb[:, :width], scalar1=rsb[:, 11:12]),
                    after=[t_rp], inc="VE")
                return (j, t_pr, psM_ret)

            def emit_C(j, t_pr):
                nonlocal prT_busy, pv_evac
                prb = sb.pr[j % 4]
                E = 2 * j + 2
                pt_toks = []
                ng4 = (E + 3) // 4
                for g4 in range(ng4):
                    n_in = min(4, E - 4 * g4)
                    pt = sb.psT[g4 % 2]
                    t_tr = None
                    for k4 in range(n_in):
                        kt = 4 * g4 + k4
                        t_tr = p.op("tensor", lambda t, kt=kt, k4=k4, pt=pt, prb=prb: t.transpose(
                            out=pt[:, 128 * k4: 128 * (k4 + 1)], in_=prb[:, tsl(kt)],
                            identity=sb.ident[:, :]),
                            after=[t_pr, psT_b2.get(g4 % 2)] +
                                  ([prT_busy] if kt == 0 else []),
                            inc="TE" if k4 == n_in - 1 else None)
                    t_pe = p.op("vector", lambda v, g4=g4, pt=pt, n_in=n_in: v.tensor_copy(
                        out=sb.prT[:, 4 * g4: 4 * g4 + n_in, :].rearrange("p a b -> p (a b)"),
                        in_=pt[:, :128 * n_in]), after=[t_tr], inc="VE")
                    psT_b2[g4 % 2] = t_pe
                    pt_toks.append(t_pe)
                t_pv = None
                ppv = sb.psPV[j % 2]
                for kt in range(E):
                    t_pv = p.op("tensor", lambda t, kt=kt, ppv=ppv: t.matmul(
                        out=ppv[:, :], lhsT=sb.vt[:, kt, :],
                        rhs=sb.prT[:, kt, :], start=(kt == 0), stop=(kt == E - 1)),
                        after=(pt_toks + [pv_evac[j % 2]]) if kt == 0 else [],
                        inc="TE" if kt == E - 1 else None)
                prT_busy = t_pv
                t_out = p.op("vector", lambda v, h=h, j=j, ppv=ppv: v.tensor_copy(
                    out=sb.attnT[:, h, tsl(j)], in_=ppv[:, :]),
                    after=[t_pv], inc="VE")
                pv_evac[j % 2] = t_out
                return pt_toks[-1]

            pipeB = {}
            pipeC = {}
            for j in range(NTJ + 4):
                if j - 4 in pipeC:
                    cj, cpr = pipeC.pop(j - 4)
                    last_pe = emit_C(cj, cpr)
                    prb_busy[cj % 4] = last_pe
                if j - 2 in pipeB:
                    bj, btoks = pipeB.pop(j - 2)
                    cj2, t_pr, psM_ret = emit_B(bj, btoks)
                    psM_busy[bj % 2] = psM_ret
                    pipeC[cj2] = (cj2, t_pr)
                if j < NTJ:
                    E = 2 * j + 2
                    nmain = 2 * j
                    prb = sb.pr[j % 4]
                    rsb = sb.rs[j % 4]
                    exp_toks = []
                    nsc = 0
                    for g4 in range((nmain + 3) // 4):
                        w = min(512, 128 * nmain - 512 * g4)
                        ps = sb.psS[g4 % 2]
                        t_mm = p.op("tensor", lambda t, g4=g4, j=j, w=w, ps=ps, par=par: t.matmul(
                            out=ps[:, :w], lhsT=sb.qn[par][:, tsl(j)],
                            rhs=sb.kh[par][:, 512 * g4: 512 * g4 + w],
                            start=True, stop=False),
                            after=[t_k, t_q, psS_busy.get(g4 % 2), vt_ready], inc=None)
                        t_mm = p.op("tensor", lambda t, g4=g4, j=j, w=w, ps=ps, par=par: t.matmul(
                            out=ps[:, :w], lhsT=sb.qr[par][:, tsl(j)],
                            rhs=sb.krot[:, 512 * g4: 512 * g4 + w],
                            start=False, stop=True),
                            after=[t_q2, consts[2]], inc="TE")
                        t_exg = p.op("scalar", lambda s, g4=g4, w=w, ps=ps, prb=prb, rsb=rsb, nsc=nsc: s.activation(
                            out=prb[:, 512 * g4: 512 * g4 + w], in_=ps[:, :w],
                            func=AF.Exp, scale=SCALE1, bias=sb.n16[:, 0:1],
                            accum_out=rsb[:, nsc:nsc + 1]),
                            after=[t_mm, prb_busy[j % 4], rs_busy[j % 4], t_n16],
                            inc="SC")
                        psS_busy[g4 % 2] = t_exg
                        exp_toks.append(t_exg)
                        nsc += 1
                    t_mmM = p.op("tensor", lambda t, j=j, par=par, nmain=nmain: t.matmul(
                        out=sb.psM[j % 2][:, :], lhsT=sb.qn[par][:, tsl(j)],
                        rhs=sb.kh[par][:, 128 * nmain: 128 * nmain + 256],
                        start=True, stop=False),
                        after=[t_k, t_q, psM_busy[j % 2], vt_ready], inc=None)
                    t_mmM = p.op("tensor", lambda t, j=j, par=par, nmain=nmain: t.matmul(
                        out=sb.psM[j % 2][:, :], lhsT=sb.qr[par][:, tsl(j)],
                        rhs=sb.krot[:, 128 * nmain: 128 * nmain + 256],
                        start=False, stop=True),
                        after=[t_q2], inc="TE")
                    newB = (j, (exp_toks, t_mmM, nsc))
                else:
                    newB = None
                if newB is not None:
                    pipeB[newB[0]] = newB
            kh_busy[par] = prT_busy
            qn_busy[par] = prT_busy
            if h + 2 < H1:
                load_head(h + 2)
        attn_done = [pv_evac[0], pv_evac[1]]
        if upto == 'C':
            p.emit(block)
            return attn_done
        p.emit(block)
        esC.close()

        NDH = 2
        dh = D // NDH
        sb.wo = es.enter_context(nc.sbuf_tensor("s3d_wo", [128, NCH, dh], BF16))
        sb.gidx2 = es.enter_context(nc.sbuf_tensor("s3d_gidx2", [1, NTJ], mybir.dt.int32))
        sb.xr = [es.enter_context(nc.sbuf_tensor(f"s3d_xr{i}", [128, D], F32))
                 for i in range(2)]
        sb.x3t = [es.enter_context(nc.sbuf_tensor(f"s3d_x3t{i}", [128, dh], F32))
                  for i in range(2)]
        sb.psO = [es.enter_context(nc.psum_tensor(f"s3d_psO{i}", [128, 512], F32))
                  for i in range(2)]
        t_gi2 = p.dma("sync", sb.gidx2[:, :], ap["gidx_x2"][:, :],
                      after=attn_done, inc="CONST")
        x2_h = ap["x2_h"]
        xr_busy = [None, None]

        def _emit_gather2(q):
            with q.register("g2r") as gr:
                q.reg_load(gr, sb.gidx2[0:1, 0:1])
                return q.dma_start(
                    out=ap["x2own_d"].rearrange("(j m) d -> j m d", m=128),
                    in_=bass.AP(x2_h, gr, [[256 * D, NTJ], [D, 128], [1, D]]),
                )
        p.sem("GT2")
        p.cnt["GT2"] += 16
        t_g2 = ("GT2", p.cnt["GT2"], None)
        p.last_dma["GT2"] = t_g2
        p.recs["gpsimd"].append((_emit_gather2, (t_gi2,), ("GT2", 16)))
        st_last = None
        wo_busy = None
        psO_busy = [None, None]
        x3_busy = [None, None]
        for half in range(NDH):
            wo_lds = []
            for c in range(NCH):
                tl = p.dma("gpsimd", sb.wo[:, c, :],
                           ap["wo1"][tsl(c), dh * half: dh * (half + 1)],
                           after=[wo_busy] + (attn_done if (half == 0 and c < 4) else []),
                           inc=f"LWU{c % 4}", chain=True)
                wo_lds.append(tl)
            wo_ready = wo_lds[-4:]
            for j in range(NTJ):
                sl = j % 2
                t_x2 = p.dma("sync", sb.xr[sl][:, :], ap["x2own_d"][tsl(j), :],
                             after=[t_g2, xr_busy[sl]] +
                                   (attn_done if (half == 0 and j < 2) else []),
                             inc=f"LDX{sl}")
                for dg in range(dh // 512):
                    t_mm = None
                    for c in range(NCH):
                        t_mm = p.op("tensor", lambda t, j=j, c=c, dg=dg: t.matmul(
                            out=sb.psO[dg % 2][:, :], lhsT=sb.attnT[:, c, tsl(j)],
                            rhs=sb.wo[:, c, 512 * dg: 512 * (dg + 1)],
                            start=(c == 0), stop=(c == NCH - 1)),
                            after=(wo_ready + [psO_busy[dg % 2]] + attn_done)
                            if c == 0 else [],
                            inc="TE" if c == NCH - 1 else None)
                    t_add = p.op("vector", lambda v, sl=sl, dg=dg, half=half: v.tensor_add(
                        out=sb.x3t[sl][:, 512 * dg: 512 * (dg + 1)],
                        in0=sb.psO[dg % 2][:, :],
                        in1=sb.xr[sl][:, dh * half + 512 * dg: dh * half + 512 * (dg + 1)]),
                        after=[t_mm, t_x2, x3_busy[sl]], inc="VE")
                    psO_busy[dg % 2] = t_add
                xr_busy[sl] = psO_busy[(dh // 512 - 1) % 2]
                t_st = p.dma("sync", ap["x3own"][tsl(j), dh * half: dh * (half + 1)],
                             sb.x3t[sl][:, :], after=[psO_busy[0], psO_busy[1]],
                             inc=f"STU{sl}")
                x3_busy[sl] = t_st
                st_last = t_st
            wo_busy = t_mm
        p.emit(block)
    return [st_last]



def emit_s4(p, block, ap, own=S // 2):
    nc = p.nc
    NT = own // 128
    NG = own // 512
    NCH = D // 128
    NFB = F_MOE // 128

    es = contextlib.ExitStack()
    with es:
        sb = SB()
        sb.ident = es.enter_context(nc.sbuf_tensor("s4_ident", [128, 128], BF16))
        sb.x = es.enter_context(nc.sbuf_tensor("s4_x", [128, 1, D], F32))
        sb.stats6 = es.enter_context(nc.sbuf_tensor("s4_st6", [128, 24], F32))
        sb.mv = es.enter_context(nc.sbuf_tensor("s4_mv", [128, 2], F32))
        sb.veps = es.enter_context(nc.sbuf_tensor("s4_veps", [128, 1], F32))
        sb.rstd = es.enter_context(nc.sbuf_tensor("s4_rstd", [128, 1], F32))
        sb.xn = es.enter_context(nc.sbuf_tensor("s4_xn", [128, D], BF16))
        sb.xnT = [es.enter_context(nc.sbuf_tensor(f"s4_xnT{i}", [128, NCH, 512], BF16))
                  for i in range(2)]
        sb.rout = es.enter_context(nc.sbuf_tensor("s4_rout", [128, NCH, 8], BF16))
        sb.rs8 = es.enter_context(nc.sbuf_tensor("s4_rs8", [128, 8], F32))
        sb.tmp8 = es.enter_context(nc.sbuf_tensor("s4_tmp8", [128, 8], F32))
        sb.t1 = es.enter_context(nc.sbuf_tensor("s4_t1", [128, 8], F32))
        sb.eq2 = es.enter_context(nc.sbuf_tensor("s4_eq2", [128, 8], F32))
        sb.m1 = es.enter_context(nc.sbuf_tensor("s4_m1", [128, 1], F32))
        sb.m2 = es.enter_context(nc.sbuf_tensor("s4_m2", [128, 1], F32))
        sb.p1 = es.enter_context(nc.sbuf_tensor("s4_p1", [128, 1], F32))
        sb.p2 = es.enter_context(nc.sbuf_tensor("s4_p2", [128, 1], F32))
        sb.w8 = [es.enter_context(nc.sbuf_tensor(f"s4_w8{i}", [128, 4, 8], F32))
                 for i in range(2)]
        sb.wg = [es.enter_context(nc.sbuf_tensor(f"s4_wg{i}", [128, NCH, F_MOE], BF16))
                 for i in range(2)]
        sb.wu = [es.enter_context(nc.sbuf_tensor(f"s4_wu{i}", [128, NCH, F_MOE], BF16))
                 for i in range(2)]
        sb.wd = [es.enter_context(nc.sbuf_tensor("s4_wd0", [128, NFB, D], BF16))] * 2
        sb.sg = es.enter_context(nc.sbuf_tensor("s4_sg", [128, NFB, 512], BF16))
        sb.h = sb.sg
        sb.acc = es.enter_context(nc.sbuf_tensor("s4_acc", [128, 4, D], F32))
        sb.psT = [es.enter_context(nc.psum_tensor("s4_psT0", [128, 128], BF16))] * 2
        sb.psR = es.enter_context(nc.psum_tensor("s4_psR", [128, 8], F32))
        sb.psG = [es.enter_context(nc.psum_tensor(f"s4_psG{i}", [128, 512], F32))
                  for i in range(2)]
        sb.psU = [es.enter_context(nc.psum_tensor(f"s4_psU{i}", [128, 512], F32))
                  for i in range(2)]
        sb.psD = [es.enter_context(nc.psum_tensor(f"s4_psD{i}", [128, 512], F32))
                  for i in range(2)]

        t_id = p.dma("sync", sb.ident[:, :], ap["ident"][:, :], inc="CONST")
        t_ro = p.dma("gpsimd", sb.rout[:, :, :],
                     ap["router_T"].rearrange("(k p) m -> p k m", p=128),
                     inc="LWW")
        consts = [t_id, t_ro]

        wload = {}
        w_busy = {}

        def ld_wd(g, e):
            war = [w_busy.get(("wd", 0)), w_busy.get(("wd", 0, "st"))]
            if g == 0:
                tl = p.dma("gpsimd", sb.wd[0][:, :, :],
                           ap["wd_T"][e].rearrange("(k p) m -> p k m", p=128),
                           after=war, inc="LWU2")
                tst = p.dma("sync",
                            ap["wd_c"][e].rearrange("(k p) m -> p k m", p=128),
                            sb.wd[0][:, :, :], after=[tl], inc="STU2")
                w_busy[("wd", 0, "st")] = tst
            else:
                tl = p.dma("gpsimd", sb.wd[0][:, :, :],
                           ap["wd_c"][e].rearrange("(k p) m -> p k m", p=128),
                           after=war, inc="LWU2")
            wload[(g, e, "wd")] = tl

        def load_w(g, e):
            par = e % 2
            first = (g == 0)
            for nm, sbuf_t, src_fp, src_c in (
                ("wg", sb.wg[par], ap["wg_T"], ap["wg_c"]),
                ("wu", sb.wu[par], ap["wu_T"], ap["wu_c"]),
            ):
                war = [w_busy.get((nm, par)), w_busy.get((nm, par, "st"))]
                if first:
                    tl = p.dma("gpsimd", sbuf_t[:, :, :],
                               src_fp[e].rearrange("(k p) m -> p k m", p=128),
                               after=war, inc=f"LWU{par}")
                    tst = p.dma("sync",
                                src_c[e].rearrange("(k p) m -> p k m", p=128),
                                sbuf_t[:, :, :], after=[tl], inc=f"STU{2 + par}")
                    wload[(g, e, nm)] = tl
                    w_busy[(nm, par, "st")] = tst
                else:
                    tl = p.dma("gpsimd", sbuf_t[:, :, :],
                               src_c[e].rearrange("(k p) m -> p k m", p=128),
                               after=war, inc=f"LWU{par}")
                    wload[(g, e, nm)] = tl
            return

        psT_busy = OneSlot()
        x_busy = [None]
        xnT_busy = [None, None]
        psR_busy = [None]
        rs8_busy = [None]
        psG_busy = [None, None]
        psU_busy = [None, None]
        psD_busy = [None, None]
        sg_busy = None
        acc_busy = [[None] * 4 for _ in range(4)]
        w8_busy = [[None] * 4, [None] * 4]
        prev = {}
        st_last = None

        load_w(0, 0)
        load_w(0, 1)
        tr_done_g = {}
        rs_done_g = {}

        def emit_ln_router(g):
            gp = g % 2
            xnT = sb.xnT[gp]
            w8 = sb.w8[gp]
            tr_done = []
            rs_done = []
            for i in range(4):
                tl = p.dma("sync", sb.x[:, 0, :], ap["x3own"][tsl(4 * g + i), :],
                           after=[x_busy[0]], inc=f"LDX{i % 2}")
                t_xn = emit_ln_tile(p, sb, sb.x[:, 0, :], tl, prev, D)
                t_ev, t_tr = emit_transpose_tile(
                    p, sb, t_xn, D,
                    lambda c, i=i, xnT=xnT: [xnT[:, c, tsl(i)]],
                    psT_busy, ident_tok=t_id,
                    extra0=[xnT_busy[gp]] if i == 0 else ())
                prev["trlast"] = t_tr
                tr_done.append(t_ev)
                t_rm = None
                for c in range(NCH):
                    t_rm = p.op("tensor", lambda t, c=c, i=i, xnT=xnT: t.matmul(
                        out=sb.psR[:, :], lhsT=xnT[:, c, tsl(i)],
                        rhs=sb.rout[:, c, :], start=(c == 0), stop=(c == NCH - 1)),
                        after=[t_ev, t_ro, psR_busy[0]] if c == 0 else [],
                        inc="TE" if c == NCH - 1 else None)
                t_e0 = p.op("vector", lambda v: v.tensor_copy(
                    out=sb.rs8[:, :], in_=sb.psR[:, :]),
                    after=[t_rm, rs8_busy[0]], inc="VE")
                psR_busy[0] = t_e0
                t_m1 = p.op("vector", lambda v: v.reduce_max(
                    out=sb.m1[:, :], in_=sb.rs8[:, :], axis=AX.X),
                    after=[t_e0], inc="VE")
                t_t1 = p.op("vector", lambda v: v.tensor_scalar(
                    out=sb.t1[:, :], in0=sb.rs8[:, :], scalar1=sb.m1[:, 0:1],
                    scalar2=None, op0=ALU.is_equal), after=[t_m1], inc="VE")
                t_tm = p.op("vector", lambda v: v.scalar_tensor_tensor(
                    out=sb.tmp8[:, :], in0=sb.t1[:, :], scalar=NEG,
                    in1=sb.rs8[:, :], op0=ALU.mult, op1=ALU.add),
                    after=[t_t1], inc="VE")
                t_m2 = p.op("vector", lambda v: v.reduce_max(
                    out=sb.m2[:, :], in_=sb.tmp8[:, :], axis=AX.X),
                    after=[t_tm], inc="VE")
                t_q2 = p.op("vector", lambda v: v.tensor_scalar(
                    out=sb.eq2[:, :], in0=sb.tmp8[:, :], scalar1=sb.m2[:, 0:1],
                    scalar2=None, op0=ALU.is_equal), after=[t_m2], inc="VE")
                t_d = p.op("vector", lambda v: v.tensor_sub(
                    out=sb.m2[:, :], in0=sb.m2[:, :], in1=sb.m1[:, 0:1]),
                    after=[t_q2], inc="VE")
                t_ex = p.op("scalar", lambda s: s.activation(
                    out=sb.p1[:, :], in_=sb.m2[:, :], func=AF.Exp),
                    after=[t_d], inc="SC")
                t_dn = p.op("vector", lambda v: v.tensor_scalar_add(
                    out=sb.p1[:, :], in0=sb.p1[:, :], scalar1=1.0),
                    after=[t_ex], inc="VE")
                t_rp = p.op("vector", lambda v: v.reciprocal(
                    out=sb.p1[:, :], in_=sb.p1[:, :]), after=[t_dn], inc="VE")
                t_p2 = p.op("vector", lambda v: v.tensor_scalar(
                    out=sb.p2[:, :], in0=sb.p1[:, :], scalar1=-1.0, scalar2=1.0,
                    op0=ALU.mult, op1=ALU.add), after=[t_rp], inc="VE")
                t_a = p.op("vector", lambda v: v.tensor_scalar_mul(
                    out=sb.t1[:, :], in0=sb.t1[:, :], scalar1=sb.p1[:, 0:1]),
                    after=[t_p2], inc="VE")
                t_b = p.op("vector", lambda v: v.tensor_scalar_mul(
                    out=sb.eq2[:, :], in0=sb.eq2[:, :], scalar1=sb.p2[:, 0:1]),
                    after=[t_a], inc="VE")
                t_w = p.op("vector", lambda v, i=i, w8=w8: v.tensor_add(
                    out=w8[:, i, :], in0=sb.t1[:, :], in1=sb.eq2[:, :]),
                    after=[t_b, w8_busy[gp][i]], inc="VE")
                rs_done.append(t_w)
                x_busy[0] = t_xn
                rs8_busy[0] = t_w
            tr_done_g[g] = tr_done
            rs_done_g[g] = rs_done

        emit_ln_router(0)
        for g in range(NG):
            gp = g % 2
            xnT = sb.xnT[gp]
            w8 = sb.w8[gp]
            tr_done = tr_done_g.pop(g)
            rs_done = rs_done_g.pop(g)
            for e in range(E_MOE):
                par = e % 2
                ld_wd(g, e)
                ne, ng2 = e + 2, g
                if ne >= E_MOE:
                    ne, ng2 = ne - E_MOE, g + 1
                gate_ready = wload[(g, e, "wg")]
                up_ready = wload[(g, e, "wu")]
                dn_ready = wload[(g, e, "wd")]
                sg_toks = []
                for fb in range(NFB):
                    pg = sb.psG[fb % 2]
                    t_g = None
                    for c in range(NCH):
                        t_g = p.op("tensor", lambda t, c=c, fb=fb, e=e, par=par, pg=pg, xnT=xnT: t.matmul(
                            out=pg[:, :], lhsT=sb.wg[par][:, c, tsl(fb)],
                            rhs=xnT[:, c, :], start=(c == 0), stop=(c == NCH - 1)),
                            after=([gate_ready, psG_busy[fb % 2]] + tr_done)
                            if c == 0 else [],
                            inc="TE" if c == NCH - 1 else None)
                    t_sg = p.op("scalar", lambda s, fb=fb, pg=pg: s.activation(
                        out=sb.sg[:, fb, :], in_=pg[:, :], func=AF.Silu),
                        after=[t_g, sg_busy], inc="SC")
                    psG_busy[fb % 2] = t_sg
                    sg_toks.append(t_sg)
                h_toks = []
                for fb in range(NFB):
                    pu = sb.psU[fb % 2]
                    t_u = None
                    for c in range(NCH):
                        t_u = p.op("tensor", lambda t, c=c, fb=fb, e=e, par=par, pu=pu, xnT=xnT: t.matmul(
                            out=pu[:, :], lhsT=sb.wu[par][:, c, tsl(fb)],
                            rhs=xnT[:, c, :], start=(c == 0), stop=(c == NCH - 1)),
                            after=([up_ready, psU_busy[fb % 2]] + tr_done)
                            if c == 0 else [],
                            inc="TE" if c == NCH - 1 else None)
                    t_h = p.op("vector", lambda v, fb=fb, pu=pu: v.tensor_tensor(
                        out=sb.h[:, fb, :], in0=sb.sg[:, fb, :], in1=pu[:, :],
                        op=ALU.mult), after=[t_u, sg_toks[fb]], inc="VE")
                    psU_busy[fb % 2] = t_h
                    h_toks.append(t_h)
                for ti in range(4):
                    for dg in range(4):
                        pd = sb.psD[dg % 2]
                        t_dm = None
                        for fb in range(NFB):
                            t_dm = p.op("tensor", lambda t, ti=ti, dg=dg, fb=fb, e=e, par=par, pd=pd: t.matmul(
                                out=pd[:, :], lhsT=sb.h[:, fb, tsl(ti)],
                                rhs=sb.wd[par][:, fb, 512 * dg: 512 * (dg + 1)],
                                start=(fb == 0), stop=(fb == NFB - 1)),
                                after=(h_toks + [dn_ready, psD_busy[dg % 2]])
                                if fb == 0 else [],
                                inc="TE" if fb == NFB - 1 else None)
                        if e == 0:
                            t_ac = p.op("vector", lambda v, ti=ti, dg=dg, e=e, pd=pd, w8=w8: v.tensor_scalar_mul(
                                out=sb.acc[:, ti, 512 * dg: 512 * (dg + 1)],
                                in0=pd[:, :], scalar1=w8[:, ti, e:e + 1]),
                                after=[t_dm, rs_done[ti], acc_busy[ti][dg]], inc="VE")
                        else:
                            t_ac = p.op("vector", lambda v, ti=ti, dg=dg, e=e, pd=pd, w8=w8: v.scalar_tensor_tensor(
                                out=sb.acc[:, ti, 512 * dg: 512 * (dg + 1)],
                                in0=pd[:, :], scalar=w8[:, ti, e:e + 1],
                                in1=sb.acc[:, ti, 512 * dg: 512 * (dg + 1)],
                                op0=ALU.mult, op1=ALU.add),
                                after=[t_dm, acc_busy[ti][dg]], inc="VE")
                        psD_busy[dg % 2] = t_ac
                        acc_busy[ti][dg] = t_ac
                sg_busy = t_dm
                w_busy[("wd", 0)] = t_dm
                w_busy[("wg", par)] = t_g
                w_busy[("wu", par)] = t_u
                if ng2 < NG:
                    load_w(ng2, ne)
                if e == 0 and g + 1 < NG:
                    emit_ln_router(g + 1)
            xnT_busy[gp] = t_dm
            for i in range(4):
                accdone = [acc_busy[i][dg] for dg in range(4)]
                trl = p.dma("sync", sb.x[:, 0, :], ap["x3own"][tsl(4 * g + i), :],
                            after=[x_busy[0]], inc=f"LDX{i % 2}")
                t_o = p.op("vector", lambda v, i=i: v.tensor_add(
                    out=sb.acc[:, i, :], in0=sb.acc[:, i, :], in1=sb.x[:, 0, :]),
                    after=accdone + [trl], inc="VE")
                x_busy[0] = t_o
                w8_busy[gp][i] = t_o
                t_st = p.dma("sync", ap["xout"][tsl(4 * g + i), :], sb.acc[:, i, :],
                             after=[t_o], inc="STT")
                acc_busy[i] = [t_st] * 4
                st_last = t_st
    return [st_last]


def build_s4_standalone(own=S // 2):
    nc = bass.Bass("TRN2", target_bir_lowering=False)
    ap = {}
    ap["x3own"] = nc.dram_tensor("x3own", [own, D], F32, kind="ExternalInput").ap()
    ap["router_T"] = nc.dram_tensor("router_T", [D, E_MOE], F32, kind="ExternalInput").ap()
    ap["wg_T"] = nc.dram_tensor("wg_T", [E_MOE, D, F_MOE], F32, kind="ExternalInput").ap()
    ap["wu_T"] = nc.dram_tensor("wu_T", [E_MOE, D, F_MOE], F32, kind="ExternalInput").ap()
    ap["wd_T"] = nc.dram_tensor("wd_T", [E_MOE, F_MOE, D], F32, kind="ExternalInput").ap()
    ap["ident"] = nc.dram_tensor("ident", [128, 128], BF16, kind="ExternalInput").ap()
    ap["xout"] = nc.dram_tensor("xout", [own, D], F32, kind="ExternalOutput").ap()
    ap["wg_c"] = nc.dram_tensor("wg_c", [E_MOE, D, F_MOE], BF16).ap()
    ap["wu_c"] = nc.dram_tensor("wu_c", [E_MOE, D, F_MOE], BF16).ap()
    ap["wd_c"] = nc.dram_tensor("wd_c", [E_MOE, F_MOE, D], BF16).ap()
    es = contextlib.ExitStack()
    with es:
        p = Prog(nc, es)
        with nc.Block() as block:
            st = emit_s4(p, block, ap, own=own)
            p.barrier("sync", st)
            p.emit(block)
    return nc



def build_s1_standalone(seq=S, d=D, h0=H0, window=WINDOW):
    nc = bass.Bass("TRN2", target_bir_lowering=False)
    SP = window + seq
    ap = {}
    ap["x0"] = nc.dram_tensor("x0", [SP, d], F32, kind="ExternalInput").ap()
    ap["wqb"] = nc.dram_tensor("wqb", [h0, d, 128], F32, kind="ExternalInput").ap()
    ap["wkb"] = nc.dram_tensor("wkb", [h0, d, 128], F32, kind="ExternalInput").ap()
    ap["wvb"] = nc.dram_tensor("wvb", [h0, d, 128], F32, kind="ExternalInput").ap()
    ap["woT"] = nc.dram_tensor("woT", [d, d], F32, kind="ExternalInput").ap()
    ap["ident"] = nc.dram_tensor("ident", [128, 128], BF16, kind="ExternalInput").ap()
    ap["tri_first_add"] = nc.dram_tensor("tri_first_add", [128, 128], BF16, kind="ExternalInput").ap()
    ap["tri_last_add"] = nc.dram_tensor("tri_last_add", [128, 128], BF16, kind="ExternalInput").ap()
    ap["npadcT"] = nc.dram_tensor("npadcT", [128, seq // 128], F32, kind="ExternalInput").ap()
    ap["x1"] = nc.dram_tensor("x1", [seq, d], F32, kind="ExternalOutput").ap()
    ap["qT"] = nc.dram_tensor("qT", [d, seq], BF16).ap()
    ap["kT"] = nc.dram_tensor("kT", [d, SP], BF16).ap()
    ap["vT"] = nc.dram_tensor("vT", [d, SP], BF16).ap()
    es = contextlib.ExitStack()
    with es:
        p = Prog(nc, es)
        with nc.Block() as block:
            st = emit_s1(p, block, ap, seq=seq, d=d, h0=h0, window=window)
            p.barrier("sync", st)
            p.emit(block)
    return nc


def build_s3_standalone(seq=S, upto='D'):
    nc = bass.Bass("TRN2", target_bir_lowering=False)
    OWN = seq // 2
    NTJ = OWN // 128
    ap = {}
    ap["x2_h"] = nc.dram_tensor("x2", [seq, D], F32, kind="ExternalInput")
    ap["x2"] = ap["x2_h"].ap()
    ap["wq_a"] = nc.dram_tensor("wq_a", [D, QLR], F32, kind="ExternalInput").ap()
    ap["gq_2d"] = nc.dram_tensor("gq_2d", [128, QLR // 128], F32, kind="ExternalInput").ap()
    ap["wq_b"] = nc.dram_tensor("wq_b", [QLR, H1 * (DN + DR)], F32, kind="ExternalInput").ap()
    ap["wkv_a"] = nc.dram_tensor("wkv_a", [D, KVLR + DR], F32, kind="ExternalInput").ap()
    ap["gkv_2d"] = nc.dram_tensor("gkv_2d", [128, KVLR // 128], F32, kind="ExternalInput").ap()
    ap["wkv_b_k"] = nc.dram_tensor("wkv_b_k", [KVLR, H1 * DN], F32, kind="ExternalInput").ap()
    ap["wkv_b_v"] = nc.dram_tensor("wkv_b_v", [KVLR, H1 * DV], F32, kind="ExternalInput").ap()
    ap["wo1"] = nc.dram_tensor("wo1", [H1 * DV, D], F32, kind="ExternalInput").ap()
    ap["ident"] = nc.dram_tensor("ident", [128, 128], BF16, kind="ExternalInput").ap()
    ap["cos_k"] = nc.dram_tensor("cos_k", [64, seq], F32, kind="ExternalInput").ap()
    ap["sin_k"] = nc.dram_tensor("sin_k", [64, seq], F32, kind="ExternalInput").ap()
    ap["cos_q"] = nc.dram_tensor("cos_q", [128, OWN], F32, kind="ExternalInput").ap()
    ap["sin_q"] = nc.dram_tensor("sin_q", [128, OWN], F32, kind="ExternalInput").ap()
    ap["mask2"] = nc.dram_tensor("mask2", [128, NTJ * 256], BF16, kind="ExternalInput").ap()
    ap["gidx_xnT"] = nc.dram_tensor("gidx_xnT", [1, NTJ], mybir.dt.int32, kind="ExternalInput").ap()
    ap["gidx_x2"] = nc.dram_tensor("gidx_x2", [1, NTJ], mybir.dt.int32, kind="ExternalInput").ap()
    ap["x3own"] = nc.dram_tensor("x3own", [OWN, D], F32, kind="ExternalOutput").ap()
    ap["xnT_d_h"] = nc.dram_tensor("xnT_d", [D, seq], BF16)
    ap["xnT_d"] = ap["xnT_d_h"].ap()
    ap["qT_d"] = nc.dram_tensor("qT_d", [H1 * (DN + DR), OWN], BF16).ap()
    ap["knope_d"] = nc.dram_tensor("knope_d", [H1 * DN, seq], BF16).ap()
    ap["vT_d"] = nc.dram_tensor("vT_d", [H1 * DV, seq], BF16).ap()
    ap["krot_d"] = nc.dram_tensor("krot_d", [64, seq], BF16).ap()
    ap["x2own_d"] = nc.dram_tensor("x2own_d", [OWN, D], F32).ap()
    es = contextlib.ExitStack()
    with es:
        p = Prog(nc, es)
        with nc.Block() as block:
            st = emit_s3(p, block, ap, seq=seq, upto=upto)
            p.barrier("sync", st)
            p.emit(block)
    return nc


def stage_s3_percore(core, seq=S):
    pr = core % 2
    OWN = seq // 2
    NTJ = OWN // 128
    d = {}
    inv = 1.0 / (10000.0 ** (np.arange(0, DR, 2) / DR))
    ang = np.arange(seq, dtype=np.float64)[None, :] * inv[:, None]
    ck = np.cos(ang).astype(np.float32)
    sk = np.sin(ang).astype(np.float32)
    d["cos_k"] = np.ascontiguousarray(np.vstack([ck, ck]))
    d["sin_k"] = np.ascontiguousarray(np.vstack([sk, sk]))
    blocks = [2 * j + pr for j in range(NTJ)]
    pos = np.concatenate([np.arange(128 * g, 128 * (g + 1)) for g in blocks])
    ango = pos[None, :].astype(np.float64) * inv[:, None]
    cq = np.cos(ango).astype(np.float32)
    sq_ = np.sin(ango).astype(np.float32)
    d["cos_q"] = np.ascontiguousarray(np.vstack([cq] * 4))
    d["sin_q"] = np.ascontiguousarray(np.vstack([sq_] * 4))
    ii = np.arange(128)[:, None]
    jj = np.arange(128)[None, :]
    tril_add = np.where(jj <= ii, 0.0, NEG).astype(np.float32)
    m2 = np.zeros((128, NTJ, 2, 128), np.float32)
    for j in range(NTJ):
        if pr == 0:
            m2[:, j, 0] = tril_add
            m2[:, j, 1] = NEG
        else:
            m2[:, j, 0] = 0.0
            m2[:, j, 1] = tril_add
    d["mask2"] = np.ascontiguousarray(m2.reshape(128, NTJ * 256)).astype(bf16)
    d["gidx_xnT"] = np.array([[128 * g for g in blocks]], np.int32)
    d["gidx_x2"] = np.full((1, NTJ), 128 * pr * D, np.int32)
    return d


def build_s2_standalone(seq=S, dff=DFF):
    nc = bass.Bass("TRN2", target_bir_lowering=False)
    ap = {}
    ap["x1"] = nc.dram_tensor("x1", [seq, D], F32, kind="ExternalInput").ap()
    ap["up_wb"] = nc.dram_tensor("up_wb", [dff // 128, D, 128], F32, kind="ExternalInput").ap()
    ap["down_wT"] = nc.dram_tensor("down_wT", [dff, D], F32, kind="ExternalInput").ap()
    ap["us2d"] = nc.dram_tensor("us2d", [128, dff // 128], F32, kind="ExternalInput").ap()
    ap["ds_t"] = nc.dram_tensor("ds_t", [128, D], F32, kind="ExternalInput").ap()
    ap["os_col"] = nc.dram_tensor("os_col", [128, 1], F32, kind="ExternalInput").ap()
    ap["ident"] = nc.dram_tensor("ident", [128, 128], BF16, kind="ExternalInput").ap()
    ap["x2"] = nc.dram_tensor("x2", [seq, D], F32, kind="ExternalOutput").ap()
    ap["up_sgn_c"] = nc.dram_tensor("up_sgn_c", [dff // 128, 128, D], BF16).ap()
    ap["down_sgn_c"] = nc.dram_tensor("down_sgn_c", [dff, D], BF16).ap()
    es = contextlib.ExitStack()
    with es:
        p = Prog(nc, es)
        with nc.Block() as block:
            st = emit_s2(p, es, ap, seq=seq, dff=dff)
            p.barrier("sync", st)
            p.emit(block)
    return nc


def build_fused():
    nc = bass.Bass("TRN2", target_bir_lowering=False,
                   dynamic_dma_scratch_size=16384)
    seq, OWN = S, S // 2
    NTJ = OWN // 128
    ap = {}
    E = "ExternalInput"
    ap["x0"] = nc.dram_tensor("x0", [WINDOW + S, D], F32, kind=E).ap()
    for nm in ("wqb", "wkb", "wvb"):
        ap[nm] = nc.dram_tensor(nm, [H0, D, 128], F32, kind=E).ap()
    ap["woT"] = nc.dram_tensor("woT", [D, D], F32, kind=E).ap()
    ap["ident"] = nc.dram_tensor("ident", [128, 128], BF16, kind=E).ap()
    ap["tri_first_add"] = nc.dram_tensor("tri_first_add", [128, 128], BF16, kind=E).ap()
    ap["tri_last_add"] = nc.dram_tensor("tri_last_add", [128, 128], BF16, kind=E).ap()
    ap["npadcT"] = nc.dram_tensor("npadcT", [128, S // 128], F32, kind=E).ap()
    ap["up_wb"] = nc.dram_tensor("up_wb", [DFF // 128, D, 128], F32, kind=E).ap()
    ap["down_wT"] = nc.dram_tensor("down_wT", [DFF, D], F32, kind=E).ap()
    ap["us2d"] = nc.dram_tensor("us2d", [128, DFF // 128], F32, kind=E).ap()
    ap["ds_t"] = nc.dram_tensor("ds_t", [128, D], F32, kind=E).ap()
    ap["os_col"] = nc.dram_tensor("os_col", [128, 1], F32, kind=E).ap()
    ap["wq_a"] = nc.dram_tensor("wq_a", [D, QLR], F32, kind=E).ap()
    ap["gq_2d"] = nc.dram_tensor("gq_2d", [128, QLR // 128], F32, kind=E).ap()
    ap["wq_b"] = nc.dram_tensor("wq_b", [QLR, H1 * (DN + DR)], F32, kind=E).ap()
    ap["wkv_a"] = nc.dram_tensor("wkv_a", [D, KVLR + DR], F32, kind=E).ap()
    ap["gkv_2d"] = nc.dram_tensor("gkv_2d", [128, KVLR // 128], F32, kind=E).ap()
    ap["wkv_b_k"] = nc.dram_tensor("wkv_b_k", [KVLR, H1 * DN], F32, kind=E).ap()
    ap["wkv_b_v"] = nc.dram_tensor("wkv_b_v", [KVLR, H1 * DV], F32, kind=E).ap()
    ap["wo1"] = nc.dram_tensor("wo1", [H1 * DV, D], F32, kind=E).ap()
    ap["cos_k"] = nc.dram_tensor("cos_k", [64, seq], F32, kind=E).ap()
    ap["sin_k"] = nc.dram_tensor("sin_k", [64, seq], F32, kind=E).ap()
    ap["cos_q"] = nc.dram_tensor("cos_q", [128, OWN], F32, kind=E).ap()
    ap["sin_q"] = nc.dram_tensor("sin_q", [128, OWN], F32, kind=E).ap()
    ap["mask2"] = nc.dram_tensor("mask2", [128, NTJ * 256], BF16, kind=E).ap()
    ap["gidx_xnT"] = nc.dram_tensor("gidx_xnT", [1, NTJ], mybir.dt.int32, kind=E).ap()
    ap["gidx_x2"] = nc.dram_tensor("gidx_x2", [1, NTJ], mybir.dt.int32, kind=E).ap()
    ap["router_T"] = nc.dram_tensor("router_T", [D, E_MOE], F32, kind=E).ap()
    ap["wg_T"] = nc.dram_tensor("wg_T", [E_MOE, D, F_MOE], F32, kind=E).ap()
    ap["wu_T"] = nc.dram_tensor("wu_T", [E_MOE, D, F_MOE], F32, kind=E).ap()
    ap["wd_T"] = nc.dram_tensor("wd_T", [E_MOE, F_MOE, D], F32, kind=E).ap()
    ap["xout"] = nc.dram_tensor("xout", [OWN, D], F32, kind="ExternalOutput").ap()
    ap["x1"] = nc.dram_tensor("x1", [S, D], F32).ap()
    ap["x2_h"] = nc.dram_tensor("x2", [S, D], F32)
    ap["x2"] = ap["x2_h"].ap()
    ap["qT"] = nc.dram_tensor("qT0", [D, S], BF16).ap()
    ap["kT"] = nc.dram_tensor("kT0", [D, WINDOW + S], BF16).ap()
    ap["vT"] = nc.dram_tensor("vT0", [D, WINDOW + S], BF16).ap()
    ap["up_sgn_c"] = nc.dram_tensor("up_sgn_c", [DFF // 128, 128, D], BF16).ap()
    ap["down_sgn_c"] = nc.dram_tensor("down_sgn_c", [DFF, D], BF16).ap()
    ap["xnT_d_h"] = nc.dram_tensor("xnT_d", [D, seq], BF16)
    ap["xnT_d"] = ap["xnT_d_h"].ap()
    ap["qT_d"] = nc.dram_tensor("qT_d", [H1 * (DN + DR), OWN], BF16).ap()
    ap["knope_d"] = nc.dram_tensor("knope_d", [H1 * DN, seq], BF16).ap()
    ap["vT_d"] = nc.dram_tensor("vT_d", [H1 * DV, seq], BF16).ap()
    ap["krot_d"] = nc.dram_tensor("krot_d", [64, seq], BF16).ap()
    ap["x2own_d"] = nc.dram_tensor("x2own_d", [OWN, D], F32).ap()
    ap["x3own"] = nc.dram_tensor("x3own", [OWN, D], F32).ap()
    ap["wg_c"] = nc.dram_tensor("wg_c", [E_MOE, D, F_MOE], BF16).ap()
    ap["wu_c"] = nc.dram_tensor("wu_c", [E_MOE, D, F_MOE], BF16).ap()
    ap["wd_c"] = nc.dram_tensor("wd_c", [E_MOE, F_MOE, D], BF16).ap()
    es = contextlib.ExitStack()
    with es:
        p = Prog(nc, es)
        with nc.Block() as block:
            st1 = emit_s1(p, block, ap)
            p.stage_barrier()
            es2 = contextlib.ExitStack()
            with es2:
                st2 = emit_s2(p, es2, ap)
                p.emit(block)
            p.stage_barrier()
            st3 = emit_s3(p, block, ap)
            p.stage_barrier()
            st4 = emit_s4(p, block, ap)
            p.barrier("sync", st1 + st2 + st3 + st4)
            p.emit(block)
    return nc


_BUILT = {}


def _get(name, fn):
    if name not in _BUILT:
        _BUILT[name] = fn()
    return _BUILT[name]


def kernel(**inputs):
    sh = stage_shared(inputs)
    cores = list(range(NCORES))
    nc = _get("fused", build_fused)
    in_maps = []
    for c in cores:
        m = {"x0": stage_x0(inputs, c)}
        for k in ("wqb", "wkb", "wvb", "woT", "ident", "tri_first_add",
                  "tri_last_add", "npadcT", "up_wb", "down_wT", "us2d", "ds_t",
                  "os_col", "wq_a", "gq_2d", "wq_b", "wkv_a", "gkv_2d",
                  "wkv_b_k", "wkv_b_v", "wo1", "router_T", "wg_T", "wu_T",
                  "wd_T"):
            m[k] = sh[k]
        m.update(stage_s3_percore(c))
        in_maps.append(m)
    res = run_bass_kernel_spmd(nc, in_maps, core_ids=cores).results
    out = np.empty((B, S, D), np.float32)
    for c in cores:
        b, pr = c // 2, c % 2
        xo = np.asarray(res[c]["xout"])
        for j in range(S // 256):
            g = 2 * j + pr
            out[b, 128 * g:128 * (g + 1)] = xo[128 * j:128 * (j + 1)]
    return out
